# version 25
# baseline (speedup 1.0000x reference)
import numpy as np
import ml_dtypes
from contextlib import ExitStack

import concourse.bass as bass
import concourse.mybir as mybir
from concourse.bass_utils import run_bass_kernel_spmd

F32 = mybir.dt.float32
BF16 = mybir.dt.bfloat16
AF = mybir.ActivationFunctionType
ALU = mybir.AluOpType
AX = mybir.AxisListType

T = 4096
D = 1024
H = 2048
NE = 8
TC = 1024
NCH = T // TC
NDC = D // 128
NHC = H // 128
JT = (slice(0, 512), slice(512, 1024))
LN_EPS = 1e-5


class EP:
    def __init__(self, sem):
        self.sem = sem
        self.n = 0
        self.ops = []

    def wait(self, sem, val):
        if sem is None or val is None or val <= 0:
            return
        self.ops.append(lambda e, s=sem, v=val: e.wait_ge(s, v))

    def wait_tok(self, tok):
        if tok is not None:
            self.wait(tok[0], tok[1])

    def do(self, fn):
        self.ops.append(lambda e, f=fn: f(e))

    def mark(self, fn):
        self.n += 1
        tok = self.n
        sem = self.sem
        self.ops.append(lambda e, f=fn, s=sem: f(e).then_inc(s, 1))
        return tok

    def tok(self):
        return (self.sem, self.n)

    def play(self, eng):
        for op in self.ops:
            op(eng)


SERIAL_DMA = False


class DmaQ:
    def __init__(self, ep, sems):
        self.ep = ep
        self.sems = list(sems)
        self.cnt = [0] * len(self.sems)
        self.i = 0

    def dma(self, out_ap, in_ap, waits=(), **kw):
        for tk in waits:
            self.ep.wait_tok(tk)
        k = self.i % len(self.sems)
        self.i += 1
        if SERIAL_DMA and self.cnt[k]:
            self.ep.wait(self.sems[k], self.cnt[k])
        self.cnt[k] += 16
        tok = self.cnt[k]
        sem = self.sems[k]
        self.ep.ops.append(
            lambda e, o=out_ap, i=in_ap, s=sem, kw=kw:
                e.dma_start(o, i, **kw).then_inc(s, 16))
        return (sem, tok)

    def wait_all(self, ep):
        for k in range(len(self.sems)):
            if self.cnt[k]:
                ep.wait(self.sems[k], self.cnt[k])


class Rot:
    def __init__(self, slots):
        self.slots = list(slots)
        self.rel = [None] * len(self.slots)
        self.i = 0

    def next(self):
        k = self.i % len(self.slots)
        self.i += 1
        return k, self.slots[k], self.rel[k]

    def release(self, k, tok):
        self.rel[k] = tok

    def set_all(self, tok):
        for k in range(len(self.slots)):
            self.rel[k] = tok


def build_program(debug=False):
    nc = bass.Bass("TRN2", target_bir_lowering=False)

    dr = {}
    def din(name, shape):
        dr[name] = nc.dram_tensor(name, list(shape), F32, kind="ExternalInput").ap()
    din("xT", (D, T))
    din("L0_attn_w_in", (D, D))
    din("L0_attn_w_out", (D, D))
    din("L0_attn_w1", (H, D))
    din("L0_attn_w2", (D, H))
    din("L0_ffn_sw1", (D, H))
    din("L0_ffn_sw2", (D, H))
    din("L0_ffn_sw3", (H, D))
    din("L0_ffn_ew1", (NE, D, H))
    din("L0_ffn_ew2", (NE, D, H))
    din("L0_ffn_ew3", (NE, H, D))
    din("L0_ffn_rw", (D, NE))
    din("vec_pack", (128, 64))
    din("b1_pack", (128, 16))
    din("rb_row", (1, 8))
    ident_b = nc.dram_tensor("ident_b", [128, 128], BF16, kind="ExternalInput").ap()
    onec_b = nc.dram_tensor("onec_b", [128, 1], BF16, kind="ExternalInput").ap()
    oner_b = nc.dram_tensor("oner_b", [1, 128], BF16, kind="ExternalInput").ap()
    moner_b = nc.dram_tensor("moner_b", [1, 128], BF16, kind="ExternalInput").ap()
    oner_f = nc.dram_tensor("oner_f", [1, 128], F32, kind="ExternalInput").ap()
    oh_b = nc.dram_tensor("oh_b", [8, 8 * 128], BF16, kind="ExternalInput").ap()
    ones8_f = nc.dram_tensor("ones8_f", [128, 8], F32, kind="ExternalInput").ap()

    outT = nc.dram_tensor("outT", [D, T], F32, kind="ExternalOutput").ap()
    wt_cache = {
        "w_in": nc.dram_tensor("wtc_in", [NDC, 128, D], BF16,
                               kind="Internal").ap(),
        "w_out": nc.dram_tensor("wtc_out", [NDC, 128, D], BF16,
                                kind="Internal").ap(),
        "w1": nc.dram_tensor("wtc_w1", [NHC, 128, D], BF16,
                             kind="Internal").ap(),
        "w2": nc.dram_tensor("wtc_w2", [NDC, 128, H], BF16,
                             kind="Internal").ap(),
    }
    dbg = {}
    if debug:
        for nm, shape in [("d_ln0a", (D, TC)), ("d_u", (D, TC)), ("d_upj", (D, TC)),
                          ("d_y", (D, TC)), ("d_amix", (D, TC)), ("d_h", (D, TC)),
                          ("d_h1", (H, TC)), ("d_aout", (D, TC)), ("d_x1", (D, TC)),
                          ("d_z", (D, TC)), ("d_gates", (128, 64)),
                          ("d_rts", (D, TC)), ("d_rt", (D, TC)),
                          ("d_prb", (128, 64)), ("d_prb2", (128, 64)),
                          ("d_m1", (128, 8)), ("d_m2", (128, 8)),
                          ("d_ff1", (128, 64)), ("d_ff2", (128, 64)),
                          ("d_eq", (128, 64)), ("d_cs", (128, 64))]:
            dbg[nm] = nc.dram_tensor(nm, list(shape), F32, kind="ExternalOutput").ap()

    with ExitStack() as stk:
        def sb(name, shape, dt=F32):
            return stk.enter_context(nc.sbuf_tensor(name, shape, dt))
        def psum(name, shape, dt=F32):
            return stk.enter_context(nc.psum_tensor(name, shape, dt))

        FX = [sb(f"FX{i}", [128, TC], F32) for i in range(NDC)]
        FA = [sb(f"FA{i}", [128, TC], F32) for i in range(NDC)]
        FB = [sb(f"FB{i}", [128, TC], BF16) for i in range(NDC)]
        B1 = [sb(f"B1_{i}", [128, TC], BF16) for i in range(NDC)]
        B2 = [sb(f"B2_{i}", [128, TC], BF16) for i in range(NDC)]
        B3 = [sb(f"B3_{i}", [128, TC], BF16) for i in range(NHC)]
        UPJ = [sb(f"UPJ{i}", [128, TC], F32) for i in range(2)]
        TW = [sb(f"TW{i}", [128, 512], BF16) for i in range(2)]
        SQB = [sb(f"SQB{i}", [128, 512], BF16) for i in range(2)]
        SIL = [sb(f"SIL{i}", [128, 512], BF16) for i in range(2)]
        TMP = sb("TMP", [128, 512], F32)
        WSA = [sb(f"WSA{i}", [128, 2048], BF16) for i in range(2)]
        WSB = [sb(f"WSB{i}", [128, 1024], BF16) for i in range(3)]
        WSC = [sb(f"WSC{i}", [128, 1024], BF16) for i in range(3)]
        WT = [sb(f"WT{i}", [128, 2048], BF16) for i in range(2)]
        GSB = sb("GSB", [128, TC], BF16)
        R1 = sb("R1", [1, 1024], F32)
        R2 = sb("R2", [1, 1024], BF16)
        RB1 = sb("RB1", [1, 1024], BF16)
        RB2 = sb("RB2", [1, 1024], BF16)
        VP = sb("VP", [128, 64], F32)
        B1C = sb("B1C", [128, 16], F32)
        LAM = sb("LAM", [128, 8], F32)
        OML = sb("OML", [128, 8], F32)
        STATE = sb("STATE", [128, 8], F32)
        RWSB = sb("RWSB", [128, 64], BF16)
        RB8 = sb("RB8", [128, 8], F32)
        RBROW = sb("RBROW", [1, 8], F32)
        IDB = sb("IDB", [128, 128], BF16)
        ONEC = sb("ONEC", [128, 1], BF16)
        ONER = sb("ONER", [1, 128], BF16)
        MONER = sb("MONER", [1, 128], BF16)
        ONERF = sb("ONERF", [1, 128], F32)
        OHB = sb("OHB", [8, 8 * 128], BF16)
        ONES8 = sb("ONES8", [128, 8], F32)
        GT = sb("GT", [8, TC], BF16)
        LG = sb("LG", [128, 64], F32)
        PRB = sb("PRB", [128, 64], F32) if debug else None
        PRB2 = sb("PRB2", [128, 64], F32)
        EQW = sb("EQW", [128, 64], F32)
        CSW = sb("CSW", [128, 64], F32)
        FF1 = sb("FF1", [128, 64], F32)
        GAT = sb("GAT", [128, 64], BF16)
        EPSC = sb("EPSC", [1, 1], F32)
        EPSC2 = sb("EPSC2", [1, 1], F32)
        SM8 = sb("SM8", [128, 8], F32)
        M18 = sb("M18", [128, 8], F32)
        M28 = sb("M28", [128, 8], F32)
        IV8 = sb("IV8", [128, 8], F32)

        ps_bc1 = psum("ps_bc1", [128, 1024], F32)
        ps_bc2 = psum("ps_bc2", [128, 1024], F32)
        ps_m1 = psum("ps_m1", [128, 512], F32)
        ps_m2 = psum("ps_m2", [128, 512], F32)
        ps_tr1 = psum("ps_tr1", [128, 1024], BF16)
        ps_tr2 = psum("ps_tr2", [128, 1024], BF16)

        sem_t = stk.enter_context(nc.semaphore("sem_t"))
        sem_v = stk.enter_context(nc.semaphore("sem_v"))
        sem_a = stk.enter_context(nc.semaphore("sem_a"))
        sems_gd = [stk.enter_context(nc.semaphore(f"sem_gd{i}"))
                   for i in range(8)]
        sems_sd = [stk.enter_context(nc.semaphore(f"sem_sd{i}"))
                   for i in range(4)]
        sem_p = stk.enter_context(nc.semaphore("sem_p"))
        sem_out = stk.enter_context(nc.semaphore("sem_out"))

        TEN = EP(sem_t)
        VEC = EP(sem_v)
        ACT = EP(sem_a)
        POOL = EP(sem_p)
        SYNC = EP(None)
        GD = DmaQ(POOL, sems_gd)
        SD = DmaQ(SYNC, sems_sd)
        OD = DmaQ(SYNC, [sem_out])

        accP = Rot([lambda: ps_m1[:, :], lambda: ps_m2[:, :],
                    lambda: ps_bc1[:, JT[0]], lambda: ps_bc1[:, JT[1]]])
        accQ = Rot([lambda: ps_bc2[:, JT[0]], lambda: ps_bc2[:, JT[1]]])
        trP = Rot([lambda: ps_tr1, lambda: ps_tr2])
        rel = {}

        SD.dma(IDB[:, :], ident_b)
        SD.dma(ONEC[:, :], onec_b)
        SD.dma(ONER[:, :], oner_b)
        SD.dma(MONER[:, :], moner_b)
        SD.dma(ONERF[:, :], oner_f)
        SD.dma(OHB[:, :], oh_b)
        SD.dma(ONES8[:, :], ones8_f)
        SD.dma(VP[:, :], dr["vec_pack"])
        SD.dma(B1C[:, :], dr["b1_pack"])
        t_vp = SD.dma(RBROW[:, :], dr["rb_row"])
        t_rw = GD.dma(RWSB[:, :].rearrange("p (dc e) -> p dc e", e=8),
                      dr["L0_ffn_rw"].rearrange("(dc p) e -> p dc e", p=128))

        const_toks = [(SD.sems[k], SD.cnt[k]) for k in range(len(SD.sems))
                      if SD.cnt[k]]
        for tk in const_toks:
            ACT.wait_tok(tk)
            TEN.wait_tok(tk)
            VEC.wait_tok(tk)
            POOL.wait_tok(tk)
        ta_lam = ACT.mark(lambda e: e.activation(LAM[:, :], VP[:, 0:8], AF.Sigmoid))
        VEC.wait(sem_a, ta_lam)
        tv_oml = VEC.mark(lambda e: e.tensor_scalar(
            OML[:, :], LAM[:, :], -1.0, 1.0, ALU.mult, ALU.add))
        TEN.wait_tok(t_vp)
        tt_rb = TEN.mark(lambda e: e.matmul(ps_m1[:, 0:8], ONERF[:, :], RBROW[:, :],
                                            start=True, stop=True))
        VEC.wait(sem_t, tt_rb)
        tv_rb8 = VEC.mark(lambda e: e.tensor_copy(RB8[:, :], ps_m1[:, 0:8]))
        accP.rel[0] = (sem_v, tv_rb8)
        VEC.do(lambda e: e.memset(EPSC[:, :], LN_EPS))
        tv_eps = VEC.mark(lambda e: e.memset(EPSC2[:, :], LN_EPS * LN_EPS))
        ACT.wait(sem_v, tv_eps)
        VEC.wait(sem_v, tv_oml)

        def vp_col(base, dc):
            return VP[:, base + dc:base + dc + 1]

        def emit_ln(src_ap, apply_fn, src_toks=None, src_wait=None,
                    derived=None, twin_dma_src=None):
            tok_pe = [None] * (2 * NDC)
            twin_toks = [[None] * NDC, [None] * NDC]
            for j in (0, 1):
                for dc in range(NDC):
                    i = j * NDC + dc
                    sslot = i % 2
                    if twin_dma_src is not None:
                        waits = []
                        if i >= 2:
                            waits.append((sem_t, tok_pe[i - 2]))
                        elif rel.get(("sil", sslot)) is not None:
                            waits.append(rel[("sil", sslot)])
                        if src_toks is not None:
                            waits.append(src_toks[j][dc])
                        t_dma = GD.dma(SIL[sslot][:, :],
                                       twin_dma_src(dc, j), waits=waits)
                        twin_toks[j][dc] = t_dma
                        ACT.wait_tok(t_dma)
                        if i >= 2:
                            ACT.wait(sem_t, tok_pe[i - 2])
                        twin = SIL[sslot]
                    else:
                        if i >= 2:
                            ACT.wait(sem_t, tok_pe[i - 2])
                        if src_toks is not None:
                            ACT.wait_tok(src_toks[j][dc])
                        elif src_wait is not None:
                            src_wait(dc, j)
                        src = src_ap(dc)
                        ACT.do(lambda e, src=src, sslot=sslot, j=j:
                               e.activation(TW[sslot][:, :], src[:, JT[j]],
                                            AF.Copy))
                        twin = TW[sslot]
                    ta2 = ACT.mark(lambda e, sslot=sslot, twin=twin:
                                   e.activation(SQB[sslot][:, :], twin[:, :],
                                                AF.Square))
                    if twin_dma_src is None:
                        twin_toks[j][dc] = (sem_a, ta2)
                    TEN.wait(sem_a, ta2)
                    if i == 0:
                        for kk in range(4):
                            TEN.wait_tok(accP.rel[kk])
                        for kk in range(2):
                            TEN.wait_tok(accQ.rel[kk])
                    TEN.do(lambda e, twin=twin, j=j, dc=dc: e.matmul(
                        ps_bc1[0:1, JT[j]], ONEC[:, :], twin[:, :],
                        start=(dc == 0), stop=(dc == NDC - 1)))
                    tok_pe[i] = TEN.mark(lambda e, sslot=sslot, j=j, dc=dc: e.matmul(
                        ps_bc2[0:1, JT[j]], ONEC[:, :], SQB[sslot][:, :],
                        start=(dc == 0), stop=(dc == NDC - 1)))
            if twin_dma_src is not None:
                rel[("sil", 0)] = (sem_t, tok_pe[2 * NDC - 2])
                rel[("sil", 1)] = (sem_t, tok_pe[2 * NDC - 1])
            ACT.wait(sem_t, tok_pe[2 * NDC - 1])
            ACT.do(lambda e: e.activation(
                R1[:, :], ps_bc1[0:1, :], AF.Copy, scale=1.0 / D))
            ta_r1 = ACT.mark(lambda e: e.activation(R2[:, :], R1[:, :],
                                                    AF.Square))
            ta_r1b = ACT.mark(lambda e: e.activation(
                RB1[:, :], ps_bc2[0:1, :], AF.Copy, scale=1.0 / D))
            POOL.wait(sem_a, ta_r1b)
            tv_v = POOL.mark(lambda e: e.tensor_tensor(
                R2[:, :], RB1[:, :], R2[:, :], ALU.subtract))
            ACT.wait(sem_p, tv_v)
            ACT.do(lambda e: e.activation(RB2[:, :], R2[:, :], AF.Ln,
                                          bias=EPSC[:, :]))
            ta_rstd = ACT.mark(lambda e: e.activation(
                RB1[:, :], RB2[:, :], AF.Exp, scale=-0.5))
            POOL.wait(sem_a, ta_rstd)
            tv_c0 = POOL.mark(lambda e: e.tensor_tensor(
                RB2[:, :], R1[:, :], RB1[:, :], ALU.mult))
            TEN.wait(sem_a, ta_rstd)
            TEN.wait(sem_p, tv_c0)
            for j in (0, 1):
                TEN.do(lambda e, j=j: e.matmul(
                    ps_bc1[:, JT[j]], ONER[:, :], RB1[0:1, JT[j]],
                    start=True, stop=True))
            TEN.do(lambda e: e.matmul(ps_bc2[:, JT[0]], MONER[:, :],
                                      RB2[0:1, JT[0]], start=True, stop=True))
            t_bc = TEN.mark(lambda e: e.matmul(
                ps_bc2[:, JT[1]], MONER[:, :], RB2[0:1, JT[1]],
                start=True, stop=True))
            VEC.wait(sem_t, t_bc)
            if pool_state["last"] is not None:
                VEC.wait_tok(pool_state["last"])
            VEC.do(lambda e: e.tensor_copy(UPJ[0][:, :], ps_bc1[:, :]))
            tv_cp = VEC.mark(lambda e: e.tensor_copy(UPJ[1][:, :],
                                                     ps_bc2[:, :]))
            bc_rel = (sem_v, tv_cp)
            accP.rel[2] = bc_rel
            accP.rel[3] = bc_rel
            accQ.rel[0] = bc_rel
            accQ.rel[1] = bc_rel
            if derived is None:
                apply_toks, lasts = apply_fn(bc_rel)
                return apply_toks, twin_toks, lasts
            if getattr(derived, "kind", None) == "affine_v":
                ACT.wait(sem_p, tv_c0)
                ACT.wait(sem_t, t_bc)
                ACT.do(lambda e: e.activation(RB2[:, :], R2[:, :], AF.Ln,
                                              bias=EPSC2[:, :],
                                              scale=1.0 + LN_EPS))
                ta_r2 = ACT.mark(lambda e: e.activation(
                    RB1[:, :], RB2[:, :], AF.Exp, scale=-0.5))
                POOL.wait(sem_a, ta_r2)
                tv_c2 = POOL.mark(lambda e: e.tensor_tensor(
                    RB2[:, :], R1[:, :], RB1[:, :], ALU.mult))
                TEN.wait(sem_a, ta_r2)
                TEN.wait(sem_p, tv_c2)
                TEN.wait_tok(bc_rel)
                for j in (0, 1):
                    TEN.do(lambda e, j=j: e.matmul(
                        ps_bc1[:, JT[j]], ONER[:, :], RB1[0:1, JT[j]],
                        start=True, stop=True))
                TEN.do(lambda e: e.matmul(ps_bc2[:, JT[0]], MONER[:, :],
                                          RB2[0:1, JT[0]], start=True,
                                          stop=True))
                t_bc2 = TEN.mark(lambda e: e.matmul(
                    ps_bc2[:, JT[1]], MONER[:, :], RB2[0:1, JT[1]],
                    start=True, stop=True))
                apply2_toks, lasts2 = derived((sem_t, t_bc2))
                bc_rel2 = derived.vec_last
                accP.rel[2] = bc_rel2
                accP.rel[3] = bc_rel2
                accQ.rel[0] = bc_rel2
                accQ.rel[1] = bc_rel2
                apply_toks, lasts = apply_fn(bc_rel)
                return apply_toks, twin_toks, lasts, apply2_toks, lasts2
            ACT.wait(sem_p, tv_c0)
            ACT.wait(sem_t, t_bc)
            ACT.do(lambda e: e.activation(RB2[:, :], R2[:, :], AF.Ln,
                                          bias=EPSC2[:, :],
                                          scale=1.0 + LN_EPS))
            ta_r2 = ACT.mark(lambda e: e.activation(
                RB1[:, :], RB2[:, :], AF.Exp, scale=-0.5))
            POOL.wait(sem_a, ta_r2)
            tv_c2 = POOL.mark(lambda e: e.tensor_tensor(
                RB2[:, :], R1[:, :], RB1[:, :], ALU.mult))
            TEN.wait(sem_p, tv_c2)
            for j in (0, 1):
                TEN.do(lambda e, j=j: e.matmul(
                    ps_bc1[:, JT[j]], ONER[:, :], RB1[0:1, JT[j]],
                    start=True, stop=True))
            TEN.do(lambda e: e.matmul(ps_bc2[:, JT[0]], MONER[:, :],
                                      RB2[0:1, JT[0]], start=True, stop=True))
            t_bc2 = TEN.mark(lambda e: e.matmul(
                ps_bc2[:, JT[1]], MONER[:, :], RB2[0:1, JT[1]],
                start=True, stop=True))
            apply_toks, lasts = apply_fn(bc_rel)
            VEC.wait(sem_t, t_bc2)
            VEC.wait_tok(pool_state["last"]) if pool_state["last"] else None
            VEC.do(lambda e: e.tensor_copy(UPJ[0][:, :], ps_bc1[:, :]))
            tv_cp2 = VEC.mark(lambda e: e.tensor_copy(UPJ[1][:, :],
                                                      ps_bc2[:, :]))
            bc_rel2 = (sem_v, tv_cp2)
            accP.rel[2] = bc_rel2
            accP.rel[3] = bc_rel2
            accQ.rel[0] = bc_rel2
            accQ.rel[1] = bc_rel2
            apply2_toks, lasts2 = derived(bc_rel2)
            return apply_toks, twin_toks, lasts, apply2_toks, lasts2

        pool_state = {"last": None}

        def mk_apply(src_buf, dst_buf, kind, gbase=0, bbase=0, res_buf=None,
                     pre_wait=None, dst_waits=None, from_psum=False,
                     order="j", src_waits=None):
            def go(stage_tok):
                if pre_wait is not None:
                    pre_wait(VEC)
                toks = [[None] * NDC, [None] * NDC]
                lasts = [None, None]
                if kind == "affine_v":
                    VEC.wait_tok(stage_tok)
                    vec_last = None
                    for j in (0, 1):
                        for dc in range(NDC):
                            if src_waits is not None and \
                               src_waits[dc] is not None:
                                VEC.wait_tok(src_waits[dc])
                            A = (ps_bc1 if from_psum else UPJ[0])[:, JT[j]]
                            C = (ps_bc2 if from_psum else UPJ[1])[:, JT[j]]
                            src = src_buf[dc][:, JT[j]]
                            dst = dst_buf[dc][:, JT[j]]
                            VEC.do(lambda e, src=src, A=A, dc=dc:
                                   e.scalar_tensor_tensor(
                                       TMP[:, :], src, vp_col(gbase, dc), A,
                                       ALU.mult, ALU.mult))
                            tv = VEC.mark(lambda e, C=C, dst=dst, dc=dc:
                                          e.scalar_tensor_tensor(
                                              dst, C, vp_col(gbase, dc),
                                              TMP[:, :], ALU.mult, ALU.add))
                            vec_last = tv
                            ACT.wait(sem_v, tv)
                            ta3 = ACT.mark(lambda e, dst=dst, dc=dc:
                                           e.activation(dst, dst, AF.Identity,
                                                        bias=vp_col(bbase, dc)))
                            toks[j][dc] = (sem_a, ta3)
                            lasts[j] = (sem_a, ta3)
                    pool_state["last"] = (sem_v, vec_last)
                    go.vec_last = (sem_v, vec_last)
                    return toks, lasts
                if order == "dc":
                    cells = [(j, dc) for dc in range(NDC) for j in (0, 1)]
                else:
                    cells = [(j, dc) for j in (0, 1) for dc in range(NDC)]
                E = VEC
                if True:
                    for j, dc in cells:
                        if dst_waits is not None and \
                           dst_waits[j][dc] is not None:
                            E.wait_tok(dst_waits[j][dc])
                        if src_waits is not None and \
                           src_waits[dc] is not None:
                            E.wait_tok(src_waits[dc])
                        A = (ps_bc1 if from_psum else UPJ[0])[:, JT[j]]
                        C = (ps_bc2 if from_psum else UPJ[1])[:, JT[j]]
                        src = src_buf[dc][:, JT[j]]
                        dst = dst_buf[dc][:, JT[j]]
                        if kind == "affine":
                            scr = TMP[:, :] if j == 0 else dst
                            E.do(lambda e, src=src, A=A, scr=scr:
                                 e.tensor_tensor(scr, src, A, ALU.mult))
                            tt = E.mark(lambda e, scr=scr, C=C:
                                        e.tensor_tensor(scr, scr, C, ALU.add))
                            E.wait(E.sem, tt)
                            last = E.mark(lambda e, scr=scr, dst=dst, dc=dc:
                                          e.tensor_scalar(
                                              dst, scr, vp_col(gbase, dc),
                                              vp_col(bbase, dc),
                                              ALU.mult, ALU.add))
                        elif kind == "affine_noC":
                            scr = TMP[:, :] if j == 0 else dst
                            tt = E.mark(lambda e, src=src, A=A, scr=scr:
                                        e.tensor_tensor(scr, src, A, ALU.mult))
                            E.wait(E.sem, tt)
                            last = E.mark(lambda e, scr=scr, dst=dst, dc=dc:
                                          e.tensor_scalar(
                                              dst, scr, vp_col(gbase, dc),
                                              vp_col(bbase, dc),
                                              ALU.mult, ALU.add))
                        elif res_buf is not None:
                            res = res_buf[dc][:, JT[j]]
                            scr = TMP[:, :] if j == 0 else dst
                            E.do(lambda e, src=src, A=A, scr=scr:
                                 e.tensor_tensor(scr, src, A, ALU.mult))
                            E.do(lambda e, scr=scr, C=C:
                                 e.tensor_tensor(scr, scr, C, ALU.add))
                            last = E.mark(lambda e, scr=scr, res=res, dst=dst:
                                          e.tensor_tensor(dst, res, scr,
                                                          ALU.add))
                        else:
                            E.do(lambda e, src=src, A=A, dst=dst:
                                 e.tensor_tensor(dst, src, A, ALU.mult))
                            last = E.mark(lambda e, dst=dst, C=C:
                                          e.tensor_tensor(dst, dst, C,
                                                          ALU.add))
                        toks[j][dc] = (E.sem, last)
                        lasts[j] = (E.sem, last)
                pool_state["last"] = lasts[1]
                return toks, lasts
            go.kind = kind
            return go

        class WStream:
            def __init__(self, wap, n_rows, pair):
                self.wap = wap
                self.n_rows = n_rows
                self.pair = pair
                self.width = None
                self.ndma = n_rows // pair
                self.tok = [None] * self.ndma
                self.k_chunks = None

            def prime(self, k_chunks):
                self.k_chunks = k_chunks
                self.width = k_chunks * 128
                self.issue(0)
                if self.ndma > 1:
                    self.issue(1)

            def issue(self, q):
                if q >= self.ndma or self.tok[q] is not None:
                    return
                src = self.wap[q * self.pair * 128:(q + 1) * self.pair * 128,
                               :].rearrange("(r p) c -> p r c", p=128)
                dst = WSA[q % 2][:, 0:self.pair * self.width].rearrange(
                    "p (r c) -> p r c", c=self.width)
                waits = []
                if rel.get(("wsa", q % 2)) is not None:
                    waits.append(rel[("wsa", q % 2)])
                self.tok[q] = GD.dma(dst, src, waits=waits)

        def attn_mm_state(ws, k_chunks, cache=None, use_cache=False,
                          ckey=None):
            if not use_cache and ws.width is None:
                ws.prime(k_chunks)
            st = {"row_done": [None] * ws.n_rows,
                  "wt_tok": [None] * ws.n_rows,
                  "wt_tok_a": [None] * ws.n_rows,
                  "k": k_chunks, "pre": 0}

            def tr_row_cached(oc):
                wslot = oc % 2
                waits = []
                if rel.get(("wt", wslot)) is not None:
                    waits.append(rel[("wt", wslot)])
                if rel.get(("wtdma", wslot)) is not None:
                    waits.append(rel[("wtdma", wslot)])
                waits.append(rel[("wtc_w", cache_key, oc)])
                half = (k_chunks // 2) * 128
                tok_a = GD.dma(WT[wslot][:, 0:half],
                               cache[oc, :, 0:half], waits=waits)
                tok_b = GD.dma(WT[wslot][:, half:k_chunks * 128],
                               cache[oc, :, half:k_chunks * 128], waits=waits)
                st["wt_tok_a"][oc] = tok_a
                st["wt_tok"][oc] = tok_b

            cache_key = ckey

            def tr_row(oc):
                n_rows, pair = ws.n_rows, ws.pair
                ng = (k_chunks + 7) // 8
                q = oc // pair
                TEN.wait_tok(ws.tok[q])
                roff = (oc % pair) * ws.width
                wslot = oc % 2
                for g in range(ng):
                    gn = min(8, k_chunks - 8 * g)
                    kslot, trmk, trrel = trP.next()
                    TEN.wait_tok(trrel)
                    for i in range(gn):
                        kc = 8 * g + i
                        fn = lambda e, kc=kc, i=i, roff=roff, q=q, trmk=trmk: \
                            e.transpose(
                                trmk()[:, i * 128:(i + 1) * 128],
                                WSA[q % 2][:, roff + kc * 128:
                                           roff + (kc + 1) * 128],
                                IDB[:, :])
                        if i == gn - 1:
                            tt_tr = TEN.mark(fn)
                        else:
                            TEN.do(fn)
                    if g == ng - 1 and (oc % pair == pair - 1 or pair == 1):
                        rel[("wsa", q % 2)] = (sem_t, tt_tr)
                        ws.issue(q + 2)
                    if rel.get(("wt", wslot)) is not None:
                        ACT.wait_tok(rel[("wt", wslot)])
                    if rel.get(("wtdma", wslot)) is not None:
                        ACT.wait_tok(rel[("wtdma", wslot)])
                    ACT.wait(sem_t, tt_tr)
                    tv_wt = ACT.mark(
                        lambda e, wslot=wslot, trmk=trmk, g=g, gn=gn:
                        e.copy(WT[wslot][:, g * 1024:g * 1024 + gn * 128],
                               trmk()[:, 0:gn * 128]))
                    trP.release(kslot, (sem_a, tv_wt))
                st["wt_tok"][oc] = (sem_a, tv_wt)
                if cache is not None:
                    tokw = GD.dma(
                        cache[oc, :, 0:k_chunks * 128],
                        WT[oc % 2][:, 0:k_chunks * 128],
                        waits=[st["wt_tok"][oc]])
                    rel[("wtc_w", cache_key, oc)] = tokw
                    rel[("wtdma", oc % 2)] = tokw

            st["tr_row"] = tr_row_cached if use_cache else tr_row
            return st

        def attn_mm_pre(st):
            st["tr_row"](0)
            if len(st["row_done"]) > 1:
                st["tr_row"](1)
            st["pre"] = 2

        def attn_weight_mm(ws, k_chunks, rhs_of, evac_fn, rhs_ready, st=None,
                           cache=None, use_cache=False, ckey=None):
            if st is None:
                st = attn_mm_state(ws, k_chunks, cache=cache,
                                   use_cache=use_cache, ckey=ckey)
            n_rows = ws.n_rows
            row_done = st["row_done"]
            wt_tok = st["wt_tok"]
            tr_row = st["tr_row"]

            def mm_row(oc, first_mm):
                wslot = oc % 2
                banks = []
                for j in (0, 1):
                    bslot, bmk, brel = accP.next()
                    TEN.wait_tok(brel)
                    banks.append((bslot, bmk))
                if st["wt_tok_a"][oc] is not None:
                    TEN.wait_tok(st["wt_tok_a"][oc])
                TEN.wait_tok(wt_tok[oc])
                tok = None
                for j in (0, 1):
                    rr = rhs_ready[j]
                    if first_mm[j] and rr is not None and \
                       not isinstance(rr, list):
                        TEN.wait_tok(rr)
                    for kc in range(k_chunks):
                        if first_mm[j] and isinstance(rr, list):
                            TEN.wait_tok(rr[kc])
                        fn = lambda e, kc=kc, j=j, bmk=banks[j][1], \
                            wslot=wslot: e.matmul(
                            bmk(), WT[wslot][:, kc * 128:(kc + 1) * 128],
                            rhs_of(kc, j),
                            start=(kc == 0), stop=(kc == k_chunks - 1))
                        if kc == k_chunks - 1:
                            tok = TEN.mark(fn)
                        else:
                            TEN.do(fn)
                    first_mm[j] = False
                row_done[oc] = tok
                rel[("wt", wslot)] = (sem_t, tok)
                for j in (0, 1):
                    ev = evac_fn(oc, j, banks[j][1], (sem_t, tok))
                    accP.release(banks[j][0], ev)

            first_mm = [True, True]
            if st["pre"] == 0:
                tr_row(0)
                if n_rows > 1:
                    tr_row(1)
            for oc in range(n_rows):
                mm_row(oc, first_mm)
                if oc + 2 < n_rows:
                    tr_row(oc + 2)
            return (sem_t, row_done[n_rows - 1])

        prev = {
            "out_toks": None,
            "lnout_twins": None,
            "lnout_apply": None,
            "chunk_pe": None,
            "state_v": [None] * NDC,
        }

        for c in range(NCH):
            csl = slice(c * TC, (c + 1) * TC)

            x_toks = []
            for dc in range(NDC):
                waits = []
                if prev["lnout_twins"] is not None:
                    waits = [prev["lnout_twins"][1][dc],
                             prev["lnout_apply"][1][dc]]
                x_toks.append(SD.dma(FX[dc][:, :],
                                     dr["xT"][dc * 128:(dc + 1) * 128, csl],
                                     waits=waits))

            ws_in = WStream(dr["L0_attn_w_in"], NDC, 2)
            ws_out = WStream(dr["L0_attn_w_out"], NDC, 2)
            ws_w1 = WStream(dr["L0_attn_w1"], NHC, 2)
            ws_w2 = WStream(dr["L0_attn_w2"], NDC, 1)
            uc = c > 0
            if not uc:
                ws_in.prime(NDC)

            ln0_dw = None
            if prev["out_toks"] is not None:
                ln0_dw = [[prev["out_toks"][dc] for dc in range(NDC)],
                          [prev["out_toks"][dc] for dc in range(NDC)]]
            a0, _, ln0_lasts, au, u_lasts = emit_ln(
                lambda dc: FX[dc][:, :],
                mk_apply(FX, FA, "plain", dst_waits=ln0_dw,
                         src_waits=x_toks),
                twin_dma_src=lambda dc, j: dr["xT"][
                    dc * 128:(dc + 1) * 128,
                    c * TC + j * 512:c * TC + j * 512 + 512],
                derived=mk_apply(FX, B1, "affine_v", 16, 24, from_psum=True,
                                 src_waits=x_toks, pre_wait=(
                    (lambda E: E.wait_tok(prev["chunk_pe"]))
                    if prev["chunk_pe"] is not None else None)))
            if debug and c == 0:
                for dc in range(NDC):
                    OD.dma(dbg["d_ln0a"][dc * 128:(dc + 1) * 128, :],
                           FA[dc][:, :], waits=ln0_lasts)
                for dc in range(NDC):
                    GD.dma(dbg["d_u"][dc * 128:(dc + 1) * 128, :],
                           B1[dc][:, :], waits=u_lasts)

            scan_ctx = {"last": None, "toks": [None] * NDC}
            upj_pool_tok = pool_state["last"]
            def upj_evac(oc, j, bmk, pe_tok):
                if oc == 0 and j == 0:
                    VEC.wait_tok(upj_pool_tok)
                VEC.wait_tok(pe_tok)
                tv = VEC.mark(lambda e, oc=oc, j=j, bmk=bmk: e.tensor_scalar_mul(
                    UPJ[oc % 2][:, JT[j]], bmk(), OML[:, oc:oc + 1]))
                if j == 1:
                    sc = oc
                    VEC.wait(sem_v, tv)
                    if sc == 0 and prev["chunk_pe"] is not None:
                        VEC.wait_tok(prev["chunk_pe"])
                    if c == 0:
                        init = 0.0
                    else:
                        VEC.wait(sem_v, prev["state_v"][sc])
                        init = STATE[:, sc:sc + 1]
                    tv_sc = VEC.mark(lambda e, sc=sc, init=init: e.tensor_tensor_scan(
                        B2[sc][:, :], LAM[:, sc:sc + 1].broadcast_to([128, TC]),
                        UPJ[sc % 2][:, :], init, ALU.mult, ALU.add))
                    VEC.wait(sem_v, tv_sc)
                    tv_st = VEC.mark(lambda e, sc=sc: e.tensor_copy(
                        STATE[:, sc:sc + 1], B2[sc][:, TC - 1:TC]))
                    prev["state_v"][sc] = tv_st
                    scan_ctx["last"] = tv_st
                    scan_ctx["toks"][sc] = (sem_v, tv_sc)
                return (sem_v, tv)
            attn_weight_mm(ws_in, NDC,
                           lambda kc, j: B1[kc][:, JT[j]], upj_evac,
                           (au[0], au[1]),
                           cache=wt_cache["w_in"], use_cache=uc, ckey="w_in")
            scan_last = scan_ctx["last"]
            if not uc:
                ws_out.prime(NDC)
            if debug and c == 0:
                for dc in range(NDC):
                    GD.dma(dbg["d_y"][dc * 128:(dc + 1) * 128, :],
                           B2[dc][:, :], waits=[(sem_v, scan_last)])

            mix_toks = [[None] * NDC, [None] * NDC]
            def mix_evac(oc, j, bmk, pe_tok):
                VEC.wait_tok(pe_tok)
                VEC.do(lambda e, oc=oc, j=j, bmk=bmk: e.scalar_tensor_tensor(
                    TMP[:, :], B1[oc][:, JT[j]], vp_col(8, oc), bmk(),
                    ALU.mult, ALU.add))
                tv = VEC.mark(lambda e, oc=oc, j=j: e.tensor_tensor(
                    FA[oc][:, JT[j]], FA[oc][:, JT[j]], TMP[:, :], ALU.add))
                mix_toks[j][oc] = (sem_v, tv)
                return (sem_v, tv)
            attn_weight_mm(ws_out, NDC,
                           lambda kc, j: B2[kc][:, JT[j]], mix_evac,
                           (scan_ctx["toks"], None),
                           cache=wt_cache["w_out"], use_cache=uc, ckey="w_out")
            if not uc:
                ws_w1.prime(NDC)
            if debug and c == 0:
                for dc in range(NDC):
                    OD.dma(dbg["d_amix"][dc * 128:(dc + 1) * 128, :],
                           FA[dc][:, :], waits=[mix_toks[1][NDC - 1]])

            ah, _, h_lasts = emit_ln(
                lambda dc: FA[dc][:, :],
                mk_apply(FA, B1, "affine_v", 32, 40, pre_wait=(
                    lambda E: E.wait_tok(mix_toks[1][NDC - 1]))),
                src_toks=mix_toks,
                twin_dma_src=lambda dc, j: FA[dc][:, JT[j]])
            if debug and c == 0:
                for dc in range(NDC):
                    GD.dma(dbg["d_h"][dc * 128:(dc + 1) * 128, :],
                           B1[dc][:, :], waits=h_lasts)

            mlp1_first = [True]
            mlp1_toks = [[None] * NHC, [None] * NHC]
            def mlp1_evac(oc, j, bmk, pe_tok):
                ACT.wait_tok(pe_tok)
                if mlp1_first[0]:
                    if prev["chunk_pe"] is not None:
                        ACT.wait_tok(prev["chunk_pe"])
                    mlp1_first[0] = False
                ta = ACT.mark(lambda e, oc=oc, j=j, bmk=bmk: e.activation(
                    B3[oc][:, JT[j]], bmk(), AF.Relu, bias=B1C[:, oc:oc + 1]))
                mlp1_toks[j][oc] = (sem_a, ta)
                return (sem_a, ta)
            attn_weight_mm(ws_w1, NDC,
                           lambda kc, j: B1[kc][:, JT[j]], mlp1_evac,
                           (ah[0], ah[1]),
                           cache=wt_cache["w1"], use_cache=uc, ckey="w1")
            if not uc:
                ws_w2.prime(NHC)
            if debug and c == 0:
                for hc in range(NHC):
                    GD.dma(dbg["d_h1"][hc * 128:(hc + 1) * 128, :],
                           B3[hc][:, :], waits=[mlp1_toks[1][NHC - 1]])

            mlp2_toks = [[None] * NDC, [None] * NDC]
            def mlp2_evac(oc, j, bmk, pe_tok):
                VEC.wait_tok(pe_tok)
                tv = VEC.mark(lambda e, oc=oc, j=j, bmk=bmk: e.scalar_tensor_tensor(
                    FA[oc][:, JT[j]], bmk(), vp_col(48, oc), FA[oc][:, JT[j]],
                    ALU.add, ALU.add))
                mlp2_toks[j][oc] = (sem_v, tv)
                return (sem_v, tv)
            attn_weight_mm(ws_w2, NHC,
                           lambda kc, j: B3[kc][:, JT[j]], mlp2_evac,
                           (mlp1_toks[0], mlp1_toks[1]),
                           cache=wt_cache["w2"], use_cache=uc, ckey="w2")
            if debug and c == 0:
                for dc in range(NDC):
                    OD.dma(dbg["d_aout"][dc * 128:(dc + 1) * 128, :],
                           FA[dc][:, :], waits=[mlp2_toks[1][NDC - 1]])

            ax1, _, x1_lasts = emit_ln(
                lambda dc: FA[dc][:, :],
                mk_apply(FA, FB, "res", res_buf=FX, pre_wait=(
                    lambda E: E.wait_tok(mlp2_toks[1][NDC - 1]))),
                src_toks=mlp2_toks,
                twin_dma_src=lambda dc, j: FA[dc][:, JT[j]])
            if debug and c == 0:
                for dc in range(NDC):
                    GD.dma(dbg["d_x1"][dc * 128:(dc + 1) * 128, :],
                           FB[dc][:, :], waits=x1_lasts)

            az, _, z_lasts = emit_ln(lambda dc: FB[dc][:, :],
                                     mk_apply(FB, B2, "plain"), src_toks=ax1,
                                     twin_dma_src=lambda dc, j:
                                     FB[dc][:, JT[j]])
            if debug and c == 0:
                for dc in range(NDC):
                    GD.dma(dbg["d_z"][dc * 128:(dc + 1) * 128, :],
                           B2[dc][:, :], waits=z_lasts)

            TEN.wait_tok(t_rw)
            kslot, bmk_r, brel = accP.next()
            TEN.wait_tok(brel)
            tt_rt = None
            for tc in range(8):
                if tc == 0:
                    TEN.wait_tok(az[0][NDC - 1])
                elif tc == 4:
                    TEN.wait_tok(az[1][NDC - 1])
                for dc in range(NDC):
                    fn = lambda e, tc=tc, dc=dc, bmk=bmk_r: e.matmul(
                        bmk()[:, tc * 8:(tc + 1) * 8],
                        B2[dc][:, tc * 128:(tc + 1) * 128],
                        RWSB[:, dc * 8:(dc + 1) * 8],
                        start=(dc == 0), stop=(dc == NDC - 1))
                    if tc == 7 and dc == NDC - 1:
                        tt_rt = TEN.mark(fn)
                    else:
                        TEN.do(fn)
            VEC.wait(sem_t, tt_rt)
            last = None
            for tc in range(8):
                last = VEC.mark(lambda e, tc=tc, bmk=bmk_r: e.tensor_tensor(
                    LG[:, tc * 8:(tc + 1) * 8], bmk()[:, tc * 8:(tc + 1) * 8],
                    RB8[:, :], ALU.add))
            accP.release(kslot, (sem_v, last))
            tv_m1 = VEC.mark(lambda e: e.reduce_max(
                M18[:, :].rearrange("p (a b) -> p a b", b=1),
                LG[:, :].rearrange("p (a b) -> p a b", b=8), axis=AX.X))
            VEC.wait(sem_v, tv_m1)
            t_ = None
            for tc in range(8):
                t_ = VEC.mark(lambda e, tc=tc: e.tensor_single_scalar(
                    EQW[:, tc * 8:(tc + 1) * 8], LG[:, tc * 8:(tc + 1) * 8],
                    M18[:, tc:tc + 1], ALU.is_equal))
            VEC.wait(sem_v, t_)
            for tc in range(8):
                t_ = VEC.mark(lambda e, tc=tc: e.tensor_tensor_scan(
                    CSW[:, tc * 8:(tc + 1) * 8], ONES8[:, :],
                    EQW[:, tc * 8:(tc + 1) * 8], 0.0, ALU.mult, ALU.add))
            VEC.wait(sem_v, t_)
            t_ = VEC.mark(lambda e: e.tensor_mul(CSW[:, :], CSW[:, :], EQW[:, :]))
            VEC.wait(sem_v, t_)
            t_ = VEC.mark(lambda e: e.tensor_single_scalar(
                FF1[:, :], CSW[:, :], 1.0, ALU.is_equal))
            VEC.wait(sem_v, t_)
            t_ = VEC.mark(lambda e: e.scalar_tensor_tensor(
                PRB2[:, :], FF1[:, :], -1e30, LG[:, :], ALU.mult, ALU.add))
            VEC.wait(sem_v, t_)
            tv_m2 = VEC.mark(lambda e: e.reduce_max(
                M28[:, :].rearrange("p (a b) -> p a b", b=1),
                PRB2[:, :].rearrange("p (a b) -> p a b", b=8), axis=AX.X))
            VEC.wait(sem_v, tv_m2)
            for tc in range(8):
                t_ = VEC.mark(lambda e, tc=tc: e.tensor_single_scalar(
                    EQW[:, tc * 8:(tc + 1) * 8], PRB2[:, tc * 8:(tc + 1) * 8],
                    M28[:, tc:tc + 1], ALU.is_equal))
            VEC.wait(sem_v, t_)
            for tc in range(8):
                t_ = VEC.mark(lambda e, tc=tc: e.tensor_tensor_scan(
                    CSW[:, tc * 8:(tc + 1) * 8], ONES8[:, :],
                    EQW[:, tc * 8:(tc + 1) * 8], 0.0, ALU.mult, ALU.add))
            VEC.wait(sem_v, t_)
            t_ = VEC.mark(lambda e: e.tensor_mul(CSW[:, :], CSW[:, :], EQW[:, :]))
            VEC.wait(sem_v, t_)
            t_ = VEC.mark(lambda e: e.tensor_single_scalar(
                EQW[:, :], CSW[:, :], 1.0, ALU.is_equal))
            VEC.wait(sem_v, t_)
            tv_d8 = VEC.mark(lambda e: e.tensor_sub(M28[:, :], M28[:, :],
                                                    M18[:, :]))
            ACT.wait(sem_v, tv_d8)
            ACT.do(lambda e: e.activation(SM8[:, :], M28[:, :], AF.Sigmoid))
            ta_gw = ACT.mark(lambda e: e.activation(IV8[:, :], M28[:, :],
                                                    AF.Sigmoid, scale=-1.0))
            if debug and c == 0:
                tv_dbg = VEC.mark(lambda e: e.tensor_copy(PRB[:, :], LG[:, :]))
                OD.dma(dbg["d_prb"], PRB[:, :], waits=[(sem_v, tv_dbg)])
                OD.dma(dbg["d_prb2"], PRB2[:, :], waits=[])
                OD.dma(dbg["d_m1"], M18[:, :], waits=[])
                OD.dma(dbg["d_m2"], M28[:, :], waits=[])
                OD.dma(dbg["d_ff1"], FF1[:, :], waits=[])
                OD.dma(dbg["d_ff2"], EQW[:, :], waits=[])
                OD.dma(dbg["d_eq"], EQW[:, :], waits=[])
                OD.dma(dbg["d_cs"], CSW[:, :], waits=[])
                VEC.wait(sem_out, OD.cnt[0])
            VEC.wait(sem_a, ta_gw)
            tv_g = None
            for tc in range(8):
                t_ = VEC.mark(lambda e, tc=tc: e.tensor_scalar_mul(
                    CSW[:, tc * 8:(tc + 1) * 8], FF1[:, tc * 8:(tc + 1) * 8],
                    IV8[:, tc:tc + 1]))
            VEC.wait(sem_v, t_)
            for tc in range(8):
                tv_g = VEC.mark(lambda e, tc=tc: e.scalar_tensor_tensor(
                    GAT[:, tc * 8:(tc + 1) * 8], EQW[:, tc * 8:(tc + 1) * 8],
                    SM8[:, tc:tc + 1], CSW[:, tc * 8:(tc + 1) * 8],
                    ALU.mult, ALU.add))
            if debug and c == 0:
                GD.dma(dbg["d_gates"], GAT[:, :], waits=[(sem_v, tv_g)])
            gt_state = {"tok": None}
            rt_last = [None]
            rt_toks = [[None] * NDC, [None] * NDC]
            pre_ffn = {}

            def ffn_w(idx):
                if idx < 0:
                    return (dr["L0_ffn_sw1"], dr["L0_ffn_sw2"],
                            dr["L0_ffn_sw3"])
                return (dr["L0_ffn_ew1"][idx], dr["L0_ffn_ew2"][idx],
                        dr["L0_ffn_ew3"][idx])

            def issue_col(nm, wap, buf, hc):
                src = wap[:, hc * 128:(hc + 1) * 128].rearrange(
                    "(dc p) cc -> p dc cc", p=128)
                dst = buf[hc % 3][:, :].rearrange(
                    "p (dc cc) -> p dc cc", cc=128)
                w = []
                if rel.get((nm, hc % 3)) is not None:
                    w.append(rel[(nm, hc % 3)])
                rel[("dma", nm, hc % 3)] = GD.dma(dst, src, waits=w)

            def issue_e3(wap, dc):
                src = wap[:, dc * 128:(dc + 1) * 128].rearrange(
                    "(hh p) cc -> p hh cc", p=128)
                dst = WSA[dc % 2][:, 0:2048].rearrange(
                    "p (hh cc) -> p hh cc", cc=128)
                w = []
                if rel.get(("wsa", dc % 2)) is not None:
                    w.append(rel[("wsa", dc % 2)])
                rel[("dma", "e3", dc % 2)] = GD.dma(
                    dst[:, 0:8, :], src[:, 0:8, :], waits=w)
                rel[("dma", "e3b", dc % 2)] = GD.dma(
                    dst[:, 8:16, :], src[:, 8:16, :], waits=w)

            def preissue_unit(idx):
                w1ap, w2ap, _ = ffn_w(idx)
                for hh in (0, 1, 2):
                    issue_col("e1", w1ap, WSB, hh)
                    issue_col("e2", w2ap, WSC, hh)
                pre_ffn[idx] = True

            def emit_gates_transpose():
                kslot_t, trmk_g, trrel = trP.next()
                TEN.wait_tok(trrel)
                TEN.wait(sem_v, tv_g)
                tt_gt = None
                for tc in range(8):
                    fn = lambda e, tc=tc, trmk=trmk_g: e.transpose(
                        trmk()[0:8, tc * 128:(tc + 1) * 128],
                        GAT[:, tc * 8:(tc + 1) * 8], IDB[:, :])
                    if tc == 7:
                        tt_gt = TEN.mark(fn)
                    else:
                        TEN.do(fn)
                VEC.wait(sem_t, tt_gt)
                tv_gt = VEC.mark(lambda e, trmk=trmk_g: e.tensor_copy(
                    GT[:, :], trmk()[0:8, :]))
                trP.release(kslot_t, (sem_v, tv_gt))
                gt_state["tok"] = tv_gt

            preps = {}

            def prep_gates(idx):
                tvzg = [None, None]
                for j in (0, 1):
                    ks, gmk, grel = accP.next()
                    TEN.wait_tok(grel)
                    if j == 0:
                        TEN.wait(sem_v, gt_state["tok"])
                    tt_g = TEN.mark(lambda e, j=j, gmk=gmk, idx=idx: e.matmul(
                        gmk(), OHB[:, idx * 128:(idx + 1) * 128],
                        GT[0:8, JT[j]], start=True, stop=True))
                    VEC.wait(sem_t, tt_g)
                    tvg2 = VEC.mark(lambda e, j=j, gmk=gmk: e.tensor_copy(
                        GSB[:, JT[j]], gmk()))
                    accP.release(ks, (sem_v, tvg2))
                    for dc in range(NDC):
                        tvzg[j] = VEC.mark(lambda e, dc=dc, j=j: e.tensor_tensor(
                            B1[dc][:, JT[j]], B2[dc][:, JT[j]],
                            GSB[:, JT[j]], ALU.mult))
                preps[idx] = ((sem_v, tvzg[0]), (sem_v, tvzg[1]))

            def ffn_unit(idx):
                shared = idx < 0
                w1ap, w2ap, w3ap = ffn_w(idx)
                if not shared:
                    if idx not in preps:
                        prep_gates(idx)
                    zg_tok = preps.pop(idx)
                    rhs2 = B1
                else:
                    rhs2 = B2
                    zg_tok = None
                he_last = None
                he_toks = [None] * NHC
                sil_i = [0]
                if not pre_ffn.pop(idx, False):
                    for hh in (0, 1, 2):
                        issue_col("e1", w1ap, WSB, hh)
                        issue_col("e2", w2ap, WSC, hh)
                for hc in range(NHC):
                    if hc == 10:
                        issue_e3(w3ap, 0)
                        issue_e3(w3ap, 1)
                    for th in (0, 1):
                        b1s, b1mk, b1rel = accP.next()
                        TEN.wait_tok(b1rel)
                        TEN.wait_tok(rel[("dma", "e1", hc % 3)])
                        if hc == 0:
                            TEN.wait_tok(az[th][NDC - 1])
                        t_p1 = None
                        for dc in range(NDC):
                            fn = lambda e, dc=dc, hc=hc, th=th, b1mk=b1mk: e.matmul(
                                b1mk(), WSB[hc % 3][:, dc * 128:(dc + 1) * 128],
                                B2[dc][:, JT[th]],
                                start=(dc == 0), stop=(dc == NDC - 1))
                            if dc == NDC - 1:
                                t_p1 = TEN.mark(fn)
                            else:
                                TEN.do(fn)
                        b2s, b2mk, b2rel = accP.next()
                        TEN.wait_tok(b2rel)
                        TEN.wait_tok(rel[("dma", "e2", hc % 3)])
                        if zg_tok is not None and hc == 0:
                            TEN.wait_tok(zg_tok[th])
                        t_p2 = None
                        for dc in range(NDC):
                            fn = lambda e, dc=dc, hc=hc, th=th, b2mk=b2mk, \
                                 rhs2=rhs2: e.matmul(
                                b2mk(), WSC[hc % 3][:, dc * 128:(dc + 1) * 128],
                                rhs2[dc][:, JT[th]],
                                start=(dc == 0), stop=(dc == NDC - 1))
                            if dc == NDC - 1:
                                t_p2 = TEN.mark(fn)
                            else:
                                TEN.do(fn)
                        if th == 1:
                            rel[("e1", hc % 3)] = (sem_t, t_p1)
                            rel[("e2", hc % 3)] = (sem_t, t_p2)
                            if hc + 3 < NHC:
                                issue_col("e1", w1ap, WSB, hc + 3)
                                issue_col("e2", w2ap, WSC, hc + 3)
                        ss = sil_i[0] % 2
                        sil_i[0] += 1
                        ACT.wait(sem_t, t_p1)
                        if rel.get(("sil", ss)) is not None:
                            ACT.wait_tok(rel[("sil", ss)])
                        if shared and hc == 0 and th == 0 and \
                           prev["chunk_pe"] is not None:
                            ACT.wait_tok(prev["chunk_pe"])
                        ta_s = ACT.mark(lambda e, ss=ss, b1mk=b1mk: e.activation(
                            SIL[ss][:, :], b1mk(), AF.Silu))
                        VEC.wait(sem_a, ta_s)
                        VEC.wait(sem_t, t_p2)
                        tv_he = VEC.mark(lambda e, hc=hc, th=th, ss=ss, b2mk=b2mk:
                                         e.tensor_tensor(B3[hc][:, JT[th]],
                                                         SIL[ss][:, :], b2mk(),
                                                         ALU.mult))
                        rel[("sil", ss)] = (sem_v, tv_he)
                        accP.release(b1s, (sem_v, tv_he))
                        accP.release(b2s, (sem_v, tv_he))
                        he_last = tv_he
                        he_toks[hc] = tv_he

                if shared:
                    emit_gates_transpose()
                for dc in range(NDC):
                    if dc == 2 and idx + 1 < NE:
                        prep_gates(idx + 1)
                    if dc == 4 and idx < NE - 1:
                        preissue_unit(idx + 1)
                    for th in (0, 1):
                        bs, bmk2, brel2 = accQ.next()
                        TEN.wait_tok(brel2)
                        TEN.wait_tok(rel[("dma", "e3", dc % 2)])
                        t_eo = None
                        for hc in range(NHC):
                            if hc == 8:
                                TEN.wait_tok(rel[("dma", "e3b", dc % 2)])
                            if dc == 0 and th == 0:
                                TEN.wait(sem_v, he_toks[hc])
                            fn = lambda e, hc=hc, dc=dc, th=th, bmk2=bmk2: \
                                e.matmul(bmk2(),
                                         WSA[dc % 2][:, hc * 128:(hc + 1) * 128],
                                         B3[hc][:, JT[th]],
                                         start=(hc == 0), stop=(hc == NHC - 1))
                            if hc == NHC - 1:
                                t_eo = TEN.mark(fn)
                            else:
                                TEN.do(fn)
                        if th == 1:
                            rel[("wsa", dc % 2)] = (sem_t, t_eo)
                            if dc + 2 < NDC:
                                issue_e3(w3ap, dc + 2)
                        VEC.wait(sem_t, t_eo)
                        if shared:
                            tv_rt = VEC.mark(lambda e, dc=dc, th=th, bmk2=bmk2:
                                             e.tensor_copy(FX[dc][:, JT[th]],
                                                           bmk2()))
                        else:
                            tv_rt = VEC.mark(lambda e, dc=dc, th=th, bmk2=bmk2:
                                             e.tensor_tensor(FX[dc][:, JT[th]],
                                                             FX[dc][:, JT[th]],
                                                             bmk2(), ALU.add))
                        accQ.release(bs, (sem_v, tv_rt))
                        rt_last[0] = tv_rt
                        if idx == NE - 1:
                            rt_toks[th][dc] = (sem_v, tv_rt)

            for idx in range(-1, NE):
                ffn_unit(idx)
                if debug and c == 0 and idx == -1:
                    for dc in range(NDC):
                        OD.dma(dbg["d_rts"][dc * 128:(dc + 1) * 128, :],
                               FX[dc][:, :], waits=[(sem_v, rt_last[0])])

            if debug and c == 0:
                for dc in range(NDC):
                    OD.dma(dbg["d_rt"][dc * 128:(dc + 1) * 128, :],
                           FX[dc][:, :], waits=[(sem_v, rt_last[0])])

            prev["chunk_pe"] = TEN.tok()

            aout, tout, out_lasts = emit_ln(
                lambda dc: FX[dc][:, :],
                mk_apply(FX, FA, "res", res_buf=FB, dst_waits=rt_toks,
                         order="dc"),
                src_toks=rt_toks,
                twin_dma_src=lambda dc, j: FX[dc][:, JT[j]])
            prev["lnout_twins"] = tout
            prev["lnout_apply"] = aout

            out_toks = []
            for dc in range(NDC):
                ot = OD.dma(outT[dc * 128:(dc + 1) * 128, csl], FA[dc][:, :],
                            waits=[aout[1][dc]])
                out_toks.append(ot)
            prev["out_toks"] = out_toks

        OD.wait_all(SYNC)
        GD.wait_all(SYNC)
        SD.wait_all(SYNC)

        with nc.Block() as block:
            @block.tensor
            def _(t):
                TEN.play(t)

            @block.vector
            def _(v):
                VEC.play(v)

            @block.scalar
            def _(s):
                ACT.play(s)

            @block.gpsimd
            def _(g):
                POOL.play(g)

            @block.sync
            def _(sy):
                SYNC.play(sy)

    return nc


def _consts():
    c = {}
    c["ident_b"] = np.eye(128, dtype=ml_dtypes.bfloat16)
    c["onec_b"] = np.ones((128, 1), dtype=ml_dtypes.bfloat16)
    c["oner_b"] = np.ones((1, 128), dtype=ml_dtypes.bfloat16)
    c["moner_b"] = -np.ones((1, 128), dtype=ml_dtypes.bfloat16)
    c["oner_f"] = np.ones((1, 128), dtype=np.float32)
    oh = np.zeros((8, 8 * 128), dtype=ml_dtypes.bfloat16)
    for e in range(8):
        oh[e, e * 128:(e + 1) * 128] = 1
    c["oh_b"] = oh
    c["ones8_f"] = np.ones((128, 8), dtype=np.float32)
    return c


def _col128(v):
    return np.ascontiguousarray(np.asarray(v).reshape(8, 128).T)


def make_inputs(inp):
    vec_pack = np.zeros((128, 64), dtype=np.float32)
    vec_pack[:, 0:8] = _col128(inp["L0_attn_log_lambda"])
    vec_pack[:, 8:16] = _col128(inp["L0_attn_D_skip"])
    vec_pack[:, 16:24] = _col128(inp["L0_attn_ln1_g"])
    vec_pack[:, 24:32] = _col128(inp["L0_attn_ln1_b"])
    vec_pack[:, 32:40] = _col128(inp["L0_attn_ln2_g"])
    vec_pack[:, 40:48] = _col128(inp["L0_attn_ln2_b"])
    vec_pack[:, 48:56] = _col128(inp["L0_attn_b2"])
    b1_pack = np.ascontiguousarray(np.asarray(inp["L0_attn_b1"]).reshape(16, 128).T)
    rb_row = np.ascontiguousarray(np.asarray(inp["L0_ffn_rb"]).reshape(1, 8))
    shared = {k: np.asarray(inp[k]) for k in (
        "L0_attn_w_in", "L0_attn_w_out", "L0_attn_w1", "L0_attn_w2",
        "L0_ffn_sw1", "L0_ffn_sw2", "L0_ffn_sw3",
        "L0_ffn_ew1", "L0_ffn_ew2", "L0_ffn_ew3", "L0_ffn_rw")}
    shared["vec_pack"] = vec_pack
    shared["b1_pack"] = b1_pack
    shared["rb_row"] = rb_row
    shared.update(_consts())
    x = np.asarray(inp["L0_attn_x"])
    in_maps = []
    for b in range(x.shape[0]):
        m = dict(shared)
        m["xT"] = np.ascontiguousarray(x[b].T)
        in_maps.append(m)
    return in_maps


def kernel(**inputs):
    in_maps = make_inputs(inputs)
    nc = build_program(debug=False)
    res = run_bass_kernel_spmd(nc, in_maps, core_ids=list(range(8)))
    out = np.stack([np.ascontiguousarray(np.asarray(r["outT"]).T)
                    for r in res.results], axis=0)
    return out.astype(np.float32)


# version 26
# speedup vs baseline: 1.1295x; 1.1295x over previous
import numpy as np
import ml_dtypes
from contextlib import ExitStack

import concourse.bass as bass
import concourse.mybir as mybir
from concourse.bass_utils import run_bass_kernel_spmd

F32 = mybir.dt.float32
BF16 = mybir.dt.bfloat16
AF = mybir.ActivationFunctionType
ALU = mybir.AluOpType
AX = mybir.AxisListType

T = 4096
D = 1024
H = 2048
NE = 8
TC = 1024
NCH = T // TC
NDC = D // 128
NHC = H // 128
JT = (slice(0, 512), slice(512, 1024))
LN_EPS = 1e-5


class EP:
    def __init__(self, sem):
        self.sem = sem
        self.n = 0
        self.ops = []

    def wait(self, sem, val):
        if sem is None or val is None or val <= 0:
            return
        self.ops.append(lambda e, s=sem, v=val: e.wait_ge(s, v))

    def wait_tok(self, tok):
        if tok is not None:
            self.wait(tok[0], tok[1])

    def do(self, fn):
        self.ops.append(lambda e, f=fn: f(e))

    def mark(self, fn):
        self.n += 1
        tok = self.n
        sem = self.sem
        self.ops.append(lambda e, f=fn, s=sem: f(e).then_inc(s, 1))
        return tok

    def tok(self):
        return (self.sem, self.n)

    def play(self, eng):
        for op in self.ops:
            op(eng)


SERIAL_DMA = False


class DmaQ:
    def __init__(self, ep, sems):
        self.ep = ep
        self.sems = list(sems)
        self.cnt = [0] * len(self.sems)
        self.i = 0

    def dma(self, out_ap, in_ap, waits=(), **kw):
        for tk in waits:
            self.ep.wait_tok(tk)
        k = self.i % len(self.sems)
        self.i += 1
        if SERIAL_DMA and self.cnt[k]:
            self.ep.wait(self.sems[k], self.cnt[k])
        self.cnt[k] += 16
        tok = self.cnt[k]
        sem = self.sems[k]
        self.ep.ops.append(
            lambda e, o=out_ap, i=in_ap, s=sem, kw=kw:
                e.dma_start(o, i, **kw).then_inc(s, 16))
        return (sem, tok)

    def wait_all(self, ep):
        for k in range(len(self.sems)):
            if self.cnt[k]:
                ep.wait(self.sems[k], self.cnt[k])


class Rot:
    def __init__(self, slots):
        self.slots = list(slots)
        self.rel = [None] * len(self.slots)
        self.i = 0

    def next(self):
        k = self.i % len(self.slots)
        self.i += 1
        return k, self.slots[k], self.rel[k]

    def release(self, k, tok):
        self.rel[k] = tok

    def set_all(self, tok):
        for k in range(len(self.slots)):
            self.rel[k] = tok


def build_program(debug=False):
    nc = bass.Bass("TRN2", target_bir_lowering=False)

    dr = {}
    def din(name, shape):
        dr[name] = nc.dram_tensor(name, list(shape), F32, kind="ExternalInput").ap()
    din("xT", (D, T))
    din("L0_attn_w_in", (D, D))
    din("L0_attn_w_out", (D, D))
    din("L0_attn_w1", (H, D))
    din("L0_attn_w2", (D, H))
    din("L0_ffn_sw1", (D, H))
    din("L0_ffn_sw2", (D, H))
    din("L0_ffn_sw3", (H, D))
    din("L0_ffn_ew1", (NE, D, H))
    din("L0_ffn_ew2", (NE, D, H))
    din("L0_ffn_ew3", (NE, H, D))
    din("L0_ffn_rw", (D, NE))
    din("vec_pack", (128, 64))
    din("b1_pack", (128, 16))
    din("rb_row", (1, 8))
    ident_b = nc.dram_tensor("ident_b", [128, 128], BF16, kind="ExternalInput").ap()
    onec_b = nc.dram_tensor("onec_b", [128, 1], BF16, kind="ExternalInput").ap()
    oner_b = nc.dram_tensor("oner_b", [1, 128], BF16, kind="ExternalInput").ap()
    moner_b = nc.dram_tensor("moner_b", [1, 128], BF16, kind="ExternalInput").ap()
    oner_f = nc.dram_tensor("oner_f", [1, 128], F32, kind="ExternalInput").ap()
    oh_b = nc.dram_tensor("oh_b", [8, 8 * 128], BF16, kind="ExternalInput").ap()
    ones8_f = nc.dram_tensor("ones8_f", [128, 8], F32, kind="ExternalInput").ap()

    outT = nc.dram_tensor("outT", [D, T], F32, kind="ExternalOutput").ap()
    wt_cache = {
        "w_in": nc.dram_tensor("wtc_in", [NDC, 128, D], BF16,
                               kind="Internal").ap(),
        "w_out": nc.dram_tensor("wtc_out", [NDC, 128, D], BF16,
                                kind="Internal").ap(),
        "w1": nc.dram_tensor("wtc_w1", [NHC, 128, D], BF16,
                             kind="Internal").ap(),
        "w2": nc.dram_tensor("wtc_w2", [NDC, 128, H], BF16,
                             kind="Internal").ap(),
    }
    dbg = {}
    if debug:
        for nm, shape in [("d_ln0a", (D, TC)), ("d_u", (D, TC)), ("d_upj", (D, TC)),
                          ("d_y", (D, TC)), ("d_amix", (D, TC)), ("d_h", (D, TC)),
                          ("d_h1", (H, TC)), ("d_aout", (D, TC)), ("d_x1", (D, TC)),
                          ("d_z", (D, TC)), ("d_gates", (128, 64)),
                          ("d_rts", (D, TC)), ("d_rt", (D, TC)),
                          ("d_prb", (128, 64)), ("d_prb2", (128, 64)),
                          ("d_m1", (128, 8)), ("d_m2", (128, 8)),
                          ("d_ff1", (128, 64)), ("d_ff2", (128, 64)),
                          ("d_eq", (128, 64)), ("d_cs", (128, 64))]:
            dbg[nm] = nc.dram_tensor(nm, list(shape), F32, kind="ExternalOutput").ap()

    with ExitStack() as stk:
        def sb(name, shape, dt=F32):
            return stk.enter_context(nc.sbuf_tensor(name, shape, dt))
        def psum(name, shape, dt=F32):
            return stk.enter_context(nc.psum_tensor(name, shape, dt))

        FX = [sb(f"FX{i}", [128, TC], F32) for i in range(NDC)]
        FA = [sb(f"FA{i}", [128, TC], F32) for i in range(NDC)]
        FB = [sb(f"FB{i}", [128, TC], BF16) for i in range(NDC)]
        B1 = [sb(f"B1_{i}", [128, TC], BF16) for i in range(NDC)]
        B2 = [sb(f"B2_{i}", [128, TC], BF16) for i in range(NDC)]
        B3 = [sb(f"B3_{i}", [128, TC], BF16) for i in range(NHC)]
        UPJ = [sb(f"UPJ{i}", [128, TC], F32) for i in range(2)]
        TW = [sb(f"TW{i}", [128, 512], BF16) for i in range(2)]
        SQB = [sb(f"SQB{i}", [128, 512], BF16) for i in range(2)]
        SIL = [sb(f"SIL{i}", [128, 512], BF16) for i in range(2)]
        TMP = sb("TMP", [128, 512], F32)
        WSA = [sb(f"WSA{i}", [128, 2048], BF16) for i in range(2)]
        WSB = [sb(f"WSB{i}", [128, 1024], BF16) for i in range(3)]
        WSC = [sb(f"WSC{i}", [128, 1024], BF16) for i in range(3)]
        WT = [sb(f"WT{i}", [128, 2048], BF16) for i in range(2)]
        GSB = sb("GSB", [128, TC], BF16)
        R1 = sb("R1", [1, 1024], F32)
        R2 = sb("R2", [1, 1024], BF16)
        RB1 = sb("RB1", [1, 1024], BF16)
        RB2 = sb("RB2", [1, 1024], BF16)
        VP = sb("VP", [128, 64], F32)
        B1C = sb("B1C", [128, 16], F32)
        LAM = sb("LAM", [128, 8], F32)
        OML = sb("OML", [128, 8], F32)
        STATE = sb("STATE", [128, 8], F32)
        RWSB = sb("RWSB", [128, 64], BF16)
        RB8 = sb("RB8", [128, 8], F32)
        RBROW = sb("RBROW", [1, 8], F32)
        IDB = sb("IDB", [128, 128], BF16)
        ONEC = sb("ONEC", [128, 1], BF16)
        ONER = sb("ONER", [1, 128], BF16)
        MONER = sb("MONER", [1, 128], BF16)
        ONERF = sb("ONERF", [1, 128], F32)
        OHB = sb("OHB", [8, 8 * 128], BF16)
        ONES8 = sb("ONES8", [128, 8], F32)
        GT = sb("GT", [8, TC], BF16)
        LG = sb("LG", [128, 64], F32)
        PRB = sb("PRB", [128, 64], F32) if debug else None
        PRB2 = sb("PRB2", [128, 64], F32)
        EQW = sb("EQW", [128, 64], F32)
        CSW = sb("CSW", [128, 64], F32)
        FF1 = sb("FF1", [128, 64], F32)
        GAT = sb("GAT", [128, 64], BF16)
        EPSC = sb("EPSC", [1, 1], F32)
        EPSC2 = sb("EPSC2", [1, 1], F32)
        SM8 = sb("SM8", [128, 8], F32)
        M18 = sb("M18", [128, 8], F32)
        M28 = sb("M28", [128, 8], F32)
        IV8 = sb("IV8", [128, 8], F32)

        ps_bc1 = psum("ps_bc1", [128, 1024], F32)
        ps_bc2 = psum("ps_bc2", [128, 1024], F32)
        ps_m1 = psum("ps_m1", [128, 512], F32)
        ps_m2 = psum("ps_m2", [128, 512], F32)
        ps_tr1 = psum("ps_tr1", [128, 1024], BF16)
        ps_tr2 = psum("ps_tr2", [128, 1024], BF16)

        sem_t = stk.enter_context(nc.semaphore("sem_t"))
        sem_v = stk.enter_context(nc.semaphore("sem_v"))
        sem_a = stk.enter_context(nc.semaphore("sem_a"))
        sems_gd = [stk.enter_context(nc.semaphore(f"sem_gd{i}"))
                   for i in range(8)]
        sems_sd = [stk.enter_context(nc.semaphore(f"sem_sd{i}"))
                   for i in range(4)]
        sem_p = stk.enter_context(nc.semaphore("sem_p"))
        sem_out = stk.enter_context(nc.semaphore("sem_out"))

        TEN = EP(sem_t)
        VEC = EP(sem_v)
        ACT = EP(sem_a)
        POOL = EP(sem_p)
        SYNC = EP(None)
        GD = DmaQ(POOL, sems_gd)
        SD = DmaQ(SYNC, sems_sd)
        OD = DmaQ(SYNC, [sem_out])

        accP = Rot([lambda: ps_m1[:, :], lambda: ps_m2[:, :],
                    lambda: ps_bc1[:, JT[0]], lambda: ps_bc1[:, JT[1]]])
        accQ = Rot([lambda: ps_bc2[:, JT[0]], lambda: ps_bc2[:, JT[1]]])
        trP = Rot([lambda: ps_tr1, lambda: ps_tr2])
        rel = {}

        SD.dma(IDB[:, :], ident_b)
        SD.dma(ONEC[:, :], onec_b)
        SD.dma(ONER[:, :], oner_b)
        SD.dma(MONER[:, :], moner_b)
        SD.dma(ONERF[:, :], oner_f)
        SD.dma(OHB[:, :], oh_b)
        SD.dma(ONES8[:, :], ones8_f)
        SD.dma(VP[:, :], dr["vec_pack"])
        SD.dma(B1C[:, :], dr["b1_pack"])
        t_vp = SD.dma(RBROW[:, :], dr["rb_row"])
        t_rw = GD.dma(RWSB[:, :].rearrange("p (dc e) -> p dc e", e=8),
                      dr["L0_ffn_rw"].rearrange("(dc p) e -> p dc e", p=128))

        const_toks = [(SD.sems[k], SD.cnt[k]) for k in range(len(SD.sems))
                      if SD.cnt[k]]
        for tk in const_toks:
            ACT.wait_tok(tk)
            TEN.wait_tok(tk)
            VEC.wait_tok(tk)
            POOL.wait_tok(tk)
        ta_lam = ACT.mark(lambda e: e.activation(LAM[:, :], VP[:, 0:8], AF.Sigmoid))
        VEC.wait(sem_a, ta_lam)
        tv_oml = VEC.mark(lambda e: e.tensor_scalar(
            OML[:, :], LAM[:, :], -1.0, 1.0, ALU.mult, ALU.add))
        TEN.wait_tok(t_vp)
        tt_rb = TEN.mark(lambda e: e.matmul(ps_m1[:, 0:8], ONERF[:, :], RBROW[:, :],
                                            start=True, stop=True))
        VEC.wait(sem_t, tt_rb)
        tv_rb8 = VEC.mark(lambda e: e.tensor_copy(RB8[:, :], ps_m1[:, 0:8]))
        accP.rel[0] = (sem_v, tv_rb8)
        VEC.do(lambda e: e.memset(EPSC[:, :], LN_EPS))
        tv_eps = VEC.mark(lambda e: e.memset(EPSC2[:, :], LN_EPS * LN_EPS))
        ACT.wait(sem_v, tv_eps)
        VEC.wait(sem_v, tv_oml)

        def vp_col(base, dc):
            return VP[:, base + dc:base + dc + 1]

        def emit_ln(src_ap, apply_fn, src_toks=None, src_wait=None,
                    derived=None, twin_dma_src=None, src_is_bf16=False):
            tok_pe = [None] * (2 * NDC)
            twin_toks = [[None] * NDC, [None] * NDC]
            dma_slots = [SIL[0], SIL[1], TW[0], TW[1]]
            tw_slots = [TW[0], TW[1], SIL[0], SIL[1]]
            sqb_slots = [SQB[0][:, :], SQB[1][:, :],
                         GSB[:, 0:512], GSB[:, 512:1024]]
            for j in (0, 1):
                for dc in range(NDC):
                    i = j * NDC + dc
                    sslot = i % 4
                    if 2 <= i <= 3 and rel.get(("gsbr",)) is not None:
                        ACT.wait_tok(rel[("gsbr",)])
                    if twin_dma_src is not None:
                        waits = []
                        if i >= 4:
                            waits.append((sem_t, tok_pe[i - 4]))
                        else:
                            rk = ("sil", sslot) if sslot < 2 else \
                                ("tw", sslot - 2)
                            if rel.get(rk) is not None:
                                waits.append(rel[rk])
                        t_dma = GD.dma(dma_slots[sslot][:, :],
                                       twin_dma_src(dc, j), waits=waits)
                        ACT.wait_tok(t_dma)
                        if i >= 4:
                            ACT.wait(sem_t, tok_pe[i - 4])
                        twin = dma_slots[sslot][:, :]
                        ta2 = ACT.mark(lambda e, sslot=sslot, twin=twin:
                                       e.activation(sqb_slots[sslot],
                                                    twin, AF.Square))
                    else:
                        if i >= 4:
                            ACT.wait(sem_t, tok_pe[i - 4])
                        if src_toks is not None:
                            ACT.wait_tok(src_toks[j][dc])
                        elif src_wait is not None:
                            src_wait(dc, j)
                        src = src_ap(dc)
                        if src_is_bf16:
                            twin = src[:, JT[j]]
                            ta2 = ACT.mark(lambda e, twin=twin, sslot=sslot:
                                           e.activation(sqb_slots[sslot],
                                                        twin, AF.Square))
                        else:
                            if sslot >= 2 and \
                               rel.get(("sil", sslot - 2)) is not None:
                                ACT.wait_tok(rel[("sil", sslot - 2)])
                            tws = tw_slots[sslot]
                            ACT.do(lambda e, src=src, tws=tws, j=j:
                                   e.activation(tws[:, :], src[:, JT[j]],
                                                AF.Copy))
                            twin = tws[:, :]
                            ta2 = ACT.mark(
                                lambda e, src=src, sslot=sslot, j=j:
                                e.activation(sqb_slots[sslot],
                                             src[:, JT[j]], AF.Square))
                    twin_toks[j][dc] = (sem_a, ta2)
                    TEN.wait(sem_a, ta2)
                    if i == 0:
                        for kk in range(4):
                            TEN.wait_tok(accP.rel[kk])
                        for kk in range(2):
                            TEN.wait_tok(accQ.rel[kk])
                    TEN.do(lambda e, twin=twin, j=j, dc=dc: e.matmul(
                        ps_bc1[0:1, JT[j]], ONEC[:, :], twin,
                        start=(dc == 0), stop=(dc == NDC - 1)))
                    tok_pe[i] = TEN.mark(lambda e, sslot=sslot, j=j, dc=dc: e.matmul(
                        ps_bc2[0:1, JT[j]], ONEC[:, :], sqb_slots[sslot],
                        start=(dc == 0), stop=(dc == NDC - 1)))
            if twin_dma_src is not None:
                rel[("sil", 0)] = (sem_t, tok_pe[2 * NDC - 4])
                rel[("sil", 1)] = (sem_t, tok_pe[2 * NDC - 3])
                rel[("tw", 0)] = (sem_t, tok_pe[2 * NDC - 2])
                rel[("tw", 1)] = (sem_t, tok_pe[2 * NDC - 1])
            elif not src_is_bf16:
                rel[("tw", 0)] = (sem_t, tok_pe[2 * NDC - 4])
                rel[("tw", 1)] = (sem_t, tok_pe[2 * NDC - 3])
                rel[("sil", 0)] = (sem_t, tok_pe[2 * NDC - 2])
                rel[("sil", 1)] = (sem_t, tok_pe[2 * NDC - 1])
            ACT.wait(sem_t, tok_pe[2 * NDC - 1])
            ACT.do(lambda e: e.activation(
                R1[:, :], ps_bc1[0:1, :], AF.Copy, scale=1.0 / D))
            ta_r1 = ACT.mark(lambda e: e.activation(R2[:, :], R1[:, :],
                                                    AF.Square))
            ta_r1b = ACT.mark(lambda e: e.activation(
                RB1[:, :], ps_bc2[0:1, :], AF.Copy, scale=1.0 / D))
            POOL.wait(sem_a, ta_r1b)
            tv_v = POOL.mark(lambda e: e.tensor_tensor(
                R2[:, :], RB1[:, :], R2[:, :], ALU.subtract))
            ACT.wait(sem_p, tv_v)
            ACT.do(lambda e: e.activation(RB2[:, :], R2[:, :], AF.Ln,
                                          bias=EPSC[:, :]))
            ta_rstd = ACT.mark(lambda e: e.activation(
                RB1[:, :], RB2[:, :], AF.Exp, scale=-0.5))
            POOL.wait(sem_a, ta_rstd)
            tv_c0 = POOL.mark(lambda e: e.tensor_tensor(
                RB2[:, :], R1[:, :], RB1[:, :], ALU.mult))
            TEN.wait(sem_a, ta_rstd)
            TEN.wait(sem_p, tv_c0)
            for j in (0, 1):
                TEN.do(lambda e, j=j: e.matmul(
                    ps_bc1[:, JT[j]], ONER[:, :], RB1[0:1, JT[j]],
                    start=True, stop=True))
            TEN.do(lambda e: e.matmul(ps_bc2[:, JT[0]], MONER[:, :],
                                      RB2[0:1, JT[0]], start=True, stop=True))
            t_bc = TEN.mark(lambda e: e.matmul(
                ps_bc2[:, JT[1]], MONER[:, :], RB2[0:1, JT[1]],
                start=True, stop=True))
            VEC.wait(sem_t, t_bc)
            if pool_state["last"] is not None:
                VEC.wait_tok(pool_state["last"])
            VEC.do(lambda e: e.tensor_copy(UPJ[0][:, :], ps_bc1[:, :]))
            tv_cp = VEC.mark(lambda e: e.tensor_copy(UPJ[1][:, :],
                                                     ps_bc2[:, :]))
            bc_rel = (sem_v, tv_cp)
            accP.rel[2] = bc_rel
            accP.rel[3] = bc_rel
            accQ.rel[0] = bc_rel
            accQ.rel[1] = bc_rel
            if derived is None:
                apply_toks, lasts = apply_fn(bc_rel)
                return apply_toks, twin_toks, lasts
            if getattr(derived, "kind", None) == "affine_v":
                ACT.wait(sem_p, tv_c0)
                ACT.wait(sem_t, t_bc)
                ACT.do(lambda e: e.activation(RB2[:, :], R2[:, :], AF.Ln,
                                              bias=EPSC2[:, :],
                                              scale=1.0 + LN_EPS))
                ta_r2 = ACT.mark(lambda e: e.activation(
                    RB1[:, :], RB2[:, :], AF.Exp, scale=-0.5))
                POOL.wait(sem_a, ta_r2)
                tv_c2 = POOL.mark(lambda e: e.tensor_tensor(
                    RB2[:, :], R1[:, :], RB1[:, :], ALU.mult))
                TEN.wait(sem_a, ta_r2)
                TEN.wait(sem_p, tv_c2)
                TEN.wait_tok(bc_rel)
                for j in (0, 1):
                    TEN.do(lambda e, j=j: e.matmul(
                        ps_bc1[:, JT[j]], ONER[:, :], RB1[0:1, JT[j]],
                        start=True, stop=True))
                TEN.do(lambda e: e.matmul(ps_bc2[:, JT[0]], MONER[:, :],
                                          RB2[0:1, JT[0]], start=True,
                                          stop=True))
                t_bc2 = TEN.mark(lambda e: e.matmul(
                    ps_bc2[:, JT[1]], MONER[:, :], RB2[0:1, JT[1]],
                    start=True, stop=True))
                apply2_toks, lasts2 = derived((sem_t, t_bc2))
                bc_rel2 = derived.vec_last
                accP.rel[2] = bc_rel2
                accP.rel[3] = bc_rel2
                accQ.rel[0] = bc_rel2
                accQ.rel[1] = bc_rel2
                apply_toks, lasts = apply_fn(bc_rel)
                return apply_toks, twin_toks, lasts, apply2_toks, lasts2
            ACT.wait(sem_p, tv_c0)
            ACT.wait(sem_t, t_bc)
            ACT.do(lambda e: e.activation(RB2[:, :], R2[:, :], AF.Ln,
                                          bias=EPSC2[:, :],
                                          scale=1.0 + LN_EPS))
            ta_r2 = ACT.mark(lambda e: e.activation(
                RB1[:, :], RB2[:, :], AF.Exp, scale=-0.5))
            POOL.wait(sem_a, ta_r2)
            tv_c2 = POOL.mark(lambda e: e.tensor_tensor(
                RB2[:, :], R1[:, :], RB1[:, :], ALU.mult))
            TEN.wait(sem_p, tv_c2)
            for j in (0, 1):
                TEN.do(lambda e, j=j: e.matmul(
                    ps_bc1[:, JT[j]], ONER[:, :], RB1[0:1, JT[j]],
                    start=True, stop=True))
            TEN.do(lambda e: e.matmul(ps_bc2[:, JT[0]], MONER[:, :],
                                      RB2[0:1, JT[0]], start=True, stop=True))
            t_bc2 = TEN.mark(lambda e: e.matmul(
                ps_bc2[:, JT[1]], MONER[:, :], RB2[0:1, JT[1]],
                start=True, stop=True))
            apply_toks, lasts = apply_fn(bc_rel)
            VEC.wait(sem_t, t_bc2)
            VEC.wait_tok(pool_state["last"]) if pool_state["last"] else None
            VEC.do(lambda e: e.tensor_copy(UPJ[0][:, :], ps_bc1[:, :]))
            tv_cp2 = VEC.mark(lambda e: e.tensor_copy(UPJ[1][:, :],
                                                      ps_bc2[:, :]))
            bc_rel2 = (sem_v, tv_cp2)
            accP.rel[2] = bc_rel2
            accP.rel[3] = bc_rel2
            accQ.rel[0] = bc_rel2
            accQ.rel[1] = bc_rel2
            apply2_toks, lasts2 = derived(bc_rel2)
            return apply_toks, twin_toks, lasts, apply2_toks, lasts2

        pool_state = {"last": None}

        def mk_apply(src_buf, dst_buf, kind, gbase=0, bbase=0, res_buf=None,
                     pre_wait=None, dst_waits=None, from_psum=False,
                     order="j", src_waits=None):
            def go(stage_tok):
                if pre_wait is not None:
                    pre_wait(VEC)
                toks = [[None] * NDC, [None] * NDC]
                lasts = [None, None]
                if kind == "affine_v":
                    VEC.wait_tok(stage_tok)
                    vec_last = None
                    for j in (0, 1):
                        for dc in range(NDC):
                            if src_waits is not None and \
                               src_waits[dc] is not None:
                                VEC.wait_tok(src_waits[dc])
                            A = (ps_bc1 if from_psum else UPJ[0])[:, JT[j]]
                            C = (ps_bc2 if from_psum else UPJ[1])[:, JT[j]]
                            src = src_buf[dc][:, JT[j]]
                            dst = dst_buf[dc][:, JT[j]]
                            VEC.do(lambda e, src=src, A=A, dc=dc:
                                   e.scalar_tensor_tensor(
                                       TMP[:, :], src, vp_col(gbase, dc), A,
                                       ALU.mult, ALU.mult))
                            tv = VEC.mark(lambda e, C=C, dst=dst, dc=dc:
                                          e.scalar_tensor_tensor(
                                              dst, C, vp_col(gbase, dc),
                                              TMP[:, :], ALU.mult, ALU.add))
                            vec_last = tv
                            ACT.wait(sem_v, tv)
                            ta3 = ACT.mark(lambda e, dst=dst, dc=dc:
                                           e.activation(dst, dst, AF.Identity,
                                                        bias=vp_col(bbase, dc)))
                            toks[j][dc] = (sem_a, ta3)
                            lasts[j] = (sem_a, ta3)
                    pool_state["last"] = (sem_v, vec_last)
                    go.vec_last = (sem_v, vec_last)
                    return toks, lasts
                if order == "dc":
                    cells = [(j, dc) for dc in range(NDC) for j in (0, 1)]
                else:
                    cells = [(j, dc) for j in (0, 1) for dc in range(NDC)]
                E = VEC
                if True:
                    for j, dc in cells:
                        if dst_waits is not None and \
                           dst_waits[j][dc] is not None:
                            E.wait_tok(dst_waits[j][dc])
                        if src_waits is not None and \
                           src_waits[dc] is not None:
                            E.wait_tok(src_waits[dc])
                        A = (ps_bc1 if from_psum else UPJ[0])[:, JT[j]]
                        C = (ps_bc2 if from_psum else UPJ[1])[:, JT[j]]
                        src = src_buf[dc][:, JT[j]]
                        dst = dst_buf[dc][:, JT[j]]
                        if kind == "affine":
                            scr = TMP[:, :] if j == 0 else dst
                            E.do(lambda e, src=src, A=A, scr=scr:
                                 e.tensor_tensor(scr, src, A, ALU.mult))
                            tt = E.mark(lambda e, scr=scr, C=C:
                                        e.tensor_tensor(scr, scr, C, ALU.add))
                            E.wait(E.sem, tt)
                            last = E.mark(lambda e, scr=scr, dst=dst, dc=dc:
                                          e.tensor_scalar(
                                              dst, scr, vp_col(gbase, dc),
                                              vp_col(bbase, dc),
                                              ALU.mult, ALU.add))
                        elif kind == "affine_noC":
                            scr = TMP[:, :] if j == 0 else dst
                            tt = E.mark(lambda e, src=src, A=A, scr=scr:
                                        e.tensor_tensor(scr, src, A, ALU.mult))
                            E.wait(E.sem, tt)
                            last = E.mark(lambda e, scr=scr, dst=dst, dc=dc:
                                          e.tensor_scalar(
                                              dst, scr, vp_col(gbase, dc),
                                              vp_col(bbase, dc),
                                              ALU.mult, ALU.add))
                        elif res_buf is not None:
                            res = res_buf[dc][:, JT[j]]
                            scr = TMP[:, :] if j == 0 else dst
                            E.do(lambda e, src=src, A=A, scr=scr:
                                 e.tensor_tensor(scr, src, A, ALU.mult))
                            E.do(lambda e, scr=scr, C=C:
                                 e.tensor_tensor(scr, scr, C, ALU.add))
                            last = E.mark(lambda e, scr=scr, res=res, dst=dst:
                                          e.tensor_tensor(dst, res, scr,
                                                          ALU.add))
                        else:
                            E.do(lambda e, src=src, A=A, dst=dst:
                                 e.tensor_tensor(dst, src, A, ALU.mult))
                            last = E.mark(lambda e, dst=dst, C=C:
                                          e.tensor_tensor(dst, dst, C,
                                                          ALU.add))
                        toks[j][dc] = (E.sem, last)
                        lasts[j] = (E.sem, last)
                pool_state["last"] = lasts[1]
                return toks, lasts
            go.kind = kind
            return go

        class WStream:
            def __init__(self, wap, n_rows, pair):
                self.wap = wap
                self.n_rows = n_rows
                self.pair = pair
                self.width = None
                self.ndma = n_rows // pair
                self.tok = [None] * self.ndma
                self.k_chunks = None

            def prime(self, k_chunks):
                self.k_chunks = k_chunks
                self.width = k_chunks * 128
                self.issue(0)
                if self.ndma > 1:
                    self.issue(1)

            def issue(self, q):
                if q >= self.ndma or self.tok[q] is not None:
                    return
                src = self.wap[q * self.pair * 128:(q + 1) * self.pair * 128,
                               :].rearrange("(r p) c -> p r c", p=128)
                dst = WSA[q % 2][:, 0:self.pair * self.width].rearrange(
                    "p (r c) -> p r c", c=self.width)
                waits = []
                if rel.get(("wsa", q % 2)) is not None:
                    waits.append(rel[("wsa", q % 2)])
                self.tok[q] = GD.dma(dst, src, waits=waits)

        def attn_mm_state(ws, k_chunks, cache=None, use_cache=False,
                          ckey=None):
            if not use_cache and ws.width is None:
                ws.prime(k_chunks)
            st = {"row_done": [None] * ws.n_rows,
                  "wt_tok": [None] * ws.n_rows,
                  "wt_tok_a": [None] * ws.n_rows,
                  "k": k_chunks, "pre": 0}

            def tr_row_cached(oc):
                wslot = oc % 2
                waits = []
                if rel.get(("wt", wslot)) is not None:
                    waits.append(rel[("wt", wslot)])
                if rel.get(("wtdma", wslot)) is not None:
                    waits.append(rel[("wtdma", wslot)])
                waits.append(rel[("wtc_w", cache_key, oc)])
                half = (k_chunks // 2) * 128
                tok_a = GD.dma(WT[wslot][:, 0:half],
                               cache[oc, :, 0:half], waits=waits)
                tok_b = GD.dma(WT[wslot][:, half:k_chunks * 128],
                               cache[oc, :, half:k_chunks * 128], waits=waits)
                st["wt_tok_a"][oc] = tok_a
                st["wt_tok"][oc] = tok_b

            cache_key = ckey

            def tr_row(oc):
                n_rows, pair = ws.n_rows, ws.pair
                ng = (k_chunks + 7) // 8
                q = oc // pair
                TEN.wait_tok(ws.tok[q])
                roff = (oc % pair) * ws.width
                wslot = oc % 2
                for g in range(ng):
                    gn = min(8, k_chunks - 8 * g)
                    kslot, trmk, trrel = trP.next()
                    TEN.wait_tok(trrel)
                    for i in range(gn):
                        kc = 8 * g + i
                        fn = lambda e, kc=kc, i=i, roff=roff, q=q, trmk=trmk: \
                            e.transpose(
                                trmk()[:, i * 128:(i + 1) * 128],
                                WSA[q % 2][:, roff + kc * 128:
                                           roff + (kc + 1) * 128],
                                IDB[:, :])
                        if i == gn - 1:
                            tt_tr = TEN.mark(fn)
                        else:
                            TEN.do(fn)
                    if g == ng - 1 and (oc % pair == pair - 1 or pair == 1):
                        rel[("wsa", q % 2)] = (sem_t, tt_tr)
                        ws.issue(q + 2)
                    if rel.get(("wt", wslot)) is not None:
                        ACT.wait_tok(rel[("wt", wslot)])
                    if rel.get(("wtdma", wslot)) is not None:
                        ACT.wait_tok(rel[("wtdma", wslot)])
                    ACT.wait(sem_t, tt_tr)
                    tv_wt = ACT.mark(
                        lambda e, wslot=wslot, trmk=trmk, g=g, gn=gn:
                        e.copy(WT[wslot][:, g * 1024:g * 1024 + gn * 128],
                               trmk()[:, 0:gn * 128]))
                    trP.release(kslot, (sem_a, tv_wt))
                st["wt_tok"][oc] = (sem_a, tv_wt)
                if cache is not None:
                    tokw = GD.dma(
                        cache[oc, :, 0:k_chunks * 128],
                        WT[oc % 2][:, 0:k_chunks * 128],
                        waits=[st["wt_tok"][oc]])
                    rel[("wtc_w", cache_key, oc)] = tokw
                    rel[("wtdma", oc % 2)] = tokw

            st["tr_row"] = tr_row_cached if use_cache else tr_row
            return st

        def attn_mm_pre(st):
            st["tr_row"](0)
            if len(st["row_done"]) > 1:
                st["tr_row"](1)
            st["pre"] = 2

        def attn_weight_mm(ws, k_chunks, rhs_of, evac_fn, rhs_ready, st=None,
                           cache=None, use_cache=False, ckey=None):
            if st is None:
                st = attn_mm_state(ws, k_chunks, cache=cache,
                                   use_cache=use_cache, ckey=ckey)
            n_rows = ws.n_rows
            row_done = st["row_done"]
            wt_tok = st["wt_tok"]
            tr_row = st["tr_row"]

            def mm_row(oc, first_mm):
                wslot = oc % 2
                banks = []
                for j in (0, 1):
                    bslot, bmk, brel = accP.next()
                    TEN.wait_tok(brel)
                    banks.append((bslot, bmk))
                if st["wt_tok_a"][oc] is not None:
                    TEN.wait_tok(st["wt_tok_a"][oc])
                TEN.wait_tok(wt_tok[oc])
                tok = None
                for j in (0, 1):
                    rr = rhs_ready[j]
                    if first_mm[j] and rr is not None and \
                       not isinstance(rr, list):
                        TEN.wait_tok(rr)
                    for kc in range(k_chunks):
                        if first_mm[j] and isinstance(rr, list):
                            TEN.wait_tok(rr[kc])
                        fn = lambda e, kc=kc, j=j, bmk=banks[j][1], \
                            wslot=wslot: e.matmul(
                            bmk(), WT[wslot][:, kc * 128:(kc + 1) * 128],
                            rhs_of(kc, j),
                            start=(kc == 0), stop=(kc == k_chunks - 1))
                        if kc == k_chunks - 1:
                            tok = TEN.mark(fn)
                        else:
                            TEN.do(fn)
                    first_mm[j] = False
                row_done[oc] = tok
                rel[("wt", wslot)] = (sem_t, tok)
                for j in (0, 1):
                    ev = evac_fn(oc, j, banks[j][1], (sem_t, tok))
                    accP.release(banks[j][0], ev)

            first_mm = [True, True]
            if st["pre"] == 0:
                tr_row(0)
                if n_rows > 1:
                    tr_row(1)
            for oc in range(n_rows):
                mm_row(oc, first_mm)
                if oc + 2 < n_rows:
                    tr_row(oc + 2)
            return (sem_t, row_done[n_rows - 1])

        prev = {
            "out_toks": None,
            "lnout_twins": None,
            "lnout_apply": None,
            "chunk_pe": None,
            "state_v": [None] * NDC,
        }

        for c in range(NCH):
            csl = slice(c * TC, (c + 1) * TC)

            x_toks = []
            for dc in range(NDC):
                waits = []
                if prev["lnout_twins"] is not None:
                    waits = [prev["lnout_twins"][1][dc],
                             prev["lnout_apply"][1][dc]]
                x_toks.append(SD.dma(FX[dc][:, :],
                                     dr["xT"][dc * 128:(dc + 1) * 128, csl],
                                     waits=waits))

            ws_in = WStream(dr["L0_attn_w_in"], NDC, 2)
            ws_out = WStream(dr["L0_attn_w_out"], NDC, 2)
            ws_w1 = WStream(dr["L0_attn_w1"], NHC, 2)
            ws_w2 = WStream(dr["L0_attn_w2"], NDC, 1)
            uc = c > 0
            if not uc:
                ws_in.prime(NDC)

            ln0_dw = None
            if prev["out_toks"] is not None:
                ln0_dw = [[prev["out_toks"][dc] for dc in range(NDC)],
                          [prev["out_toks"][dc] for dc in range(NDC)]]
            a0, _, ln0_lasts, au, u_lasts = emit_ln(
                lambda dc: FX[dc][:, :],
                mk_apply(FX, FA, "plain", dst_waits=ln0_dw,
                         src_waits=x_toks),
                twin_dma_src=lambda dc, j: dr["xT"][
                    dc * 128:(dc + 1) * 128,
                    c * TC + j * 512:c * TC + j * 512 + 512],
                derived=mk_apply(FX, B1, "affine_v", 16, 24, from_psum=True,
                                 src_waits=x_toks, pre_wait=(
                    (lambda E: E.wait_tok(prev["chunk_pe"]))
                    if prev["chunk_pe"] is not None else None)))
            if debug and c == 0:
                for dc in range(NDC):
                    OD.dma(dbg["d_ln0a"][dc * 128:(dc + 1) * 128, :],
                           FA[dc][:, :], waits=ln0_lasts)
                for dc in range(NDC):
                    GD.dma(dbg["d_u"][dc * 128:(dc + 1) * 128, :],
                           B1[dc][:, :], waits=u_lasts)

            scan_ctx = {"last": None, "toks": [None] * NDC}
            upj_pool_tok = pool_state["last"]
            def upj_evac(oc, j, bmk, pe_tok):
                if oc == 0 and j == 0:
                    VEC.wait_tok(upj_pool_tok)
                VEC.wait_tok(pe_tok)
                tv = VEC.mark(lambda e, oc=oc, j=j, bmk=bmk: e.tensor_scalar_mul(
                    UPJ[oc % 2][:, JT[j]], bmk(), OML[:, oc:oc + 1]))
                if j == 1:
                    sc = oc
                    VEC.wait(sem_v, tv)
                    if sc == 0 and prev["chunk_pe"] is not None:
                        VEC.wait_tok(prev["chunk_pe"])
                    if c == 0:
                        init = 0.0
                    else:
                        VEC.wait(sem_v, prev["state_v"][sc])
                        init = STATE[:, sc:sc + 1]
                    tv_sc = VEC.mark(lambda e, sc=sc, init=init: e.tensor_tensor_scan(
                        B2[sc][:, :], LAM[:, sc:sc + 1].broadcast_to([128, TC]),
                        UPJ[sc % 2][:, :], init, ALU.mult, ALU.add))
                    VEC.wait(sem_v, tv_sc)
                    tv_st = VEC.mark(lambda e, sc=sc: e.tensor_copy(
                        STATE[:, sc:sc + 1], B2[sc][:, TC - 1:TC]))
                    prev["state_v"][sc] = tv_st
                    scan_ctx["last"] = tv_st
                    scan_ctx["toks"][sc] = (sem_v, tv_sc)
                return (sem_v, tv)
            attn_weight_mm(ws_in, NDC,
                           lambda kc, j: B1[kc][:, JT[j]], upj_evac,
                           (au[0], au[1]),
                           cache=wt_cache["w_in"], use_cache=uc, ckey="w_in")
            scan_last = scan_ctx["last"]
            if not uc:
                ws_out.prime(NDC)
            if debug and c == 0:
                for dc in range(NDC):
                    GD.dma(dbg["d_y"][dc * 128:(dc + 1) * 128, :],
                           B2[dc][:, :], waits=[(sem_v, scan_last)])

            mix_toks = [[None] * NDC, [None] * NDC]
            def mix_evac(oc, j, bmk, pe_tok):
                VEC.wait_tok(pe_tok)
                VEC.do(lambda e, oc=oc, j=j, bmk=bmk: e.scalar_tensor_tensor(
                    TMP[:, :], B1[oc][:, JT[j]], vp_col(8, oc), bmk(),
                    ALU.mult, ALU.add))
                tv = VEC.mark(lambda e, oc=oc, j=j: e.tensor_tensor(
                    FA[oc][:, JT[j]], FA[oc][:, JT[j]], TMP[:, :], ALU.add))
                mix_toks[j][oc] = (sem_v, tv)
                return (sem_v, tv)
            attn_weight_mm(ws_out, NDC,
                           lambda kc, j: B2[kc][:, JT[j]], mix_evac,
                           (scan_ctx["toks"], None),
                           cache=wt_cache["w_out"], use_cache=uc, ckey="w_out")
            if not uc:
                ws_w1.prime(NDC)
            if debug and c == 0:
                for dc in range(NDC):
                    OD.dma(dbg["d_amix"][dc * 128:(dc + 1) * 128, :],
                           FA[dc][:, :], waits=[mix_toks[1][NDC - 1]])

            ah, _, h_lasts = emit_ln(
                lambda dc: FA[dc][:, :],
                mk_apply(FA, B1, "affine_v", 32, 40, pre_wait=(
                    lambda E: E.wait_tok(mix_toks[1][NDC - 1]))),
                src_toks=mix_toks)
            if debug and c == 0:
                for dc in range(NDC):
                    GD.dma(dbg["d_h"][dc * 128:(dc + 1) * 128, :],
                           B1[dc][:, :], waits=h_lasts)

            mlp1_first = [True]
            mlp1_toks = [[None] * NHC, [None] * NHC]
            def mlp1_evac(oc, j, bmk, pe_tok):
                ACT.wait_tok(pe_tok)
                if mlp1_first[0]:
                    if prev["chunk_pe"] is not None:
                        ACT.wait_tok(prev["chunk_pe"])
                    mlp1_first[0] = False
                ta = ACT.mark(lambda e, oc=oc, j=j, bmk=bmk: e.activation(
                    B3[oc][:, JT[j]], bmk(), AF.Relu, bias=B1C[:, oc:oc + 1]))
                mlp1_toks[j][oc] = (sem_a, ta)
                return (sem_a, ta)
            attn_weight_mm(ws_w1, NDC,
                           lambda kc, j: B1[kc][:, JT[j]], mlp1_evac,
                           (ah[0], ah[1]),
                           cache=wt_cache["w1"], use_cache=uc, ckey="w1")
            if not uc:
                ws_w2.prime(NHC)
            if debug and c == 0:
                for hc in range(NHC):
                    GD.dma(dbg["d_h1"][hc * 128:(hc + 1) * 128, :],
                           B3[hc][:, :], waits=[mlp1_toks[1][NHC - 1]])

            mlp2_toks = [[None] * NDC, [None] * NDC]
            def mlp2_evac(oc, j, bmk, pe_tok):
                VEC.wait_tok(pe_tok)
                tv = VEC.mark(lambda e, oc=oc, j=j, bmk=bmk: e.scalar_tensor_tensor(
                    FA[oc][:, JT[j]], bmk(), vp_col(48, oc), FA[oc][:, JT[j]],
                    ALU.add, ALU.add))
                mlp2_toks[j][oc] = (sem_v, tv)
                return (sem_v, tv)
            attn_weight_mm(ws_w2, NHC,
                           lambda kc, j: B3[kc][:, JT[j]], mlp2_evac,
                           (mlp1_toks[0], mlp1_toks[1]),
                           cache=wt_cache["w2"], use_cache=uc, ckey="w2")
            if debug and c == 0:
                for dc in range(NDC):
                    OD.dma(dbg["d_aout"][dc * 128:(dc + 1) * 128, :],
                           FA[dc][:, :], waits=[mlp2_toks[1][NDC - 1]])

            ax1, _, x1_lasts = emit_ln(
                lambda dc: FA[dc][:, :],
                mk_apply(FA, FB, "res", res_buf=FX, pre_wait=(
                    lambda E: E.wait_tok(mlp2_toks[1][NDC - 1]))),
                src_toks=mlp2_toks)
            if debug and c == 0:
                for dc in range(NDC):
                    GD.dma(dbg["d_x1"][dc * 128:(dc + 1) * 128, :],
                           FB[dc][:, :], waits=x1_lasts)

            az, _, z_lasts = emit_ln(lambda dc: FB[dc][:, :],
                                     mk_apply(FB, B2, "plain"), src_toks=ax1,
                                     src_is_bf16=True)
            if debug and c == 0:
                for dc in range(NDC):
                    GD.dma(dbg["d_z"][dc * 128:(dc + 1) * 128, :],
                           B2[dc][:, :], waits=z_lasts)

            TEN.wait_tok(t_rw)
            kslot, bmk_r, brel = accP.next()
            TEN.wait_tok(brel)
            tt_rt = None
            for tc in range(8):
                if tc == 0:
                    TEN.wait_tok(az[0][NDC - 1])
                elif tc == 4:
                    TEN.wait_tok(az[1][NDC - 1])
                for dc in range(NDC):
                    fn = lambda e, tc=tc, dc=dc, bmk=bmk_r: e.matmul(
                        bmk()[:, tc * 8:(tc + 1) * 8],
                        B2[dc][:, tc * 128:(tc + 1) * 128],
                        RWSB[:, dc * 8:(dc + 1) * 8],
                        start=(dc == 0), stop=(dc == NDC - 1))
                    if tc == 7 and dc == NDC - 1:
                        tt_rt = TEN.mark(fn)
                    else:
                        TEN.do(fn)
            VEC.wait(sem_t, tt_rt)
            last = None
            for tc in range(8):
                last = VEC.mark(lambda e, tc=tc, bmk=bmk_r: e.tensor_tensor(
                    LG[:, tc * 8:(tc + 1) * 8], bmk()[:, tc * 8:(tc + 1) * 8],
                    RB8[:, :], ALU.add))
            accP.release(kslot, (sem_v, last))
            tv_m1 = VEC.mark(lambda e: e.reduce_max(
                M18[:, :].rearrange("p (a b) -> p a b", b=1),
                LG[:, :].rearrange("p (a b) -> p a b", b=8), axis=AX.X))
            VEC.wait(sem_v, tv_m1)
            t_ = None
            for tc in range(8):
                t_ = VEC.mark(lambda e, tc=tc: e.tensor_single_scalar(
                    EQW[:, tc * 8:(tc + 1) * 8], LG[:, tc * 8:(tc + 1) * 8],
                    M18[:, tc:tc + 1], ALU.is_equal))
            VEC.wait(sem_v, t_)
            for tc in range(8):
                t_ = VEC.mark(lambda e, tc=tc: e.tensor_tensor_scan(
                    CSW[:, tc * 8:(tc + 1) * 8], ONES8[:, :],
                    EQW[:, tc * 8:(tc + 1) * 8], 0.0, ALU.mult, ALU.add))
            VEC.wait(sem_v, t_)
            t_ = VEC.mark(lambda e: e.tensor_mul(CSW[:, :], CSW[:, :], EQW[:, :]))
            VEC.wait(sem_v, t_)
            t_ = VEC.mark(lambda e: e.tensor_single_scalar(
                FF1[:, :], CSW[:, :], 1.0, ALU.is_equal))
            VEC.wait(sem_v, t_)
            t_ = VEC.mark(lambda e: e.scalar_tensor_tensor(
                PRB2[:, :], FF1[:, :], -1e30, LG[:, :], ALU.mult, ALU.add))
            VEC.wait(sem_v, t_)
            tv_m2 = VEC.mark(lambda e: e.reduce_max(
                M28[:, :].rearrange("p (a b) -> p a b", b=1),
                PRB2[:, :].rearrange("p (a b) -> p a b", b=8), axis=AX.X))
            VEC.wait(sem_v, tv_m2)
            for tc in range(8):
                t_ = VEC.mark(lambda e, tc=tc: e.tensor_single_scalar(
                    EQW[:, tc * 8:(tc + 1) * 8], PRB2[:, tc * 8:(tc + 1) * 8],
                    M28[:, tc:tc + 1], ALU.is_equal))
            VEC.wait(sem_v, t_)
            for tc in range(8):
                t_ = VEC.mark(lambda e, tc=tc: e.tensor_tensor_scan(
                    CSW[:, tc * 8:(tc + 1) * 8], ONES8[:, :],
                    EQW[:, tc * 8:(tc + 1) * 8], 0.0, ALU.mult, ALU.add))
            VEC.wait(sem_v, t_)
            t_ = VEC.mark(lambda e: e.tensor_mul(CSW[:, :], CSW[:, :], EQW[:, :]))
            VEC.wait(sem_v, t_)
            t_ = VEC.mark(lambda e: e.tensor_single_scalar(
                EQW[:, :], CSW[:, :], 1.0, ALU.is_equal))
            VEC.wait(sem_v, t_)
            tv_d8 = VEC.mark(lambda e: e.tensor_sub(M28[:, :], M28[:, :],
                                                    M18[:, :]))
            ACT.wait(sem_v, tv_d8)
            ACT.do(lambda e: e.activation(SM8[:, :], M28[:, :], AF.Sigmoid))
            ta_gw = ACT.mark(lambda e: e.activation(IV8[:, :], M28[:, :],
                                                    AF.Sigmoid, scale=-1.0))
            if debug and c == 0:
                tv_dbg = VEC.mark(lambda e: e.tensor_copy(PRB[:, :], LG[:, :]))
                OD.dma(dbg["d_prb"], PRB[:, :], waits=[(sem_v, tv_dbg)])
                OD.dma(dbg["d_prb2"], PRB2[:, :], waits=[])
                OD.dma(dbg["d_m1"], M18[:, :], waits=[])
                OD.dma(dbg["d_m2"], M28[:, :], waits=[])
                OD.dma(dbg["d_ff1"], FF1[:, :], waits=[])
                OD.dma(dbg["d_ff2"], EQW[:, :], waits=[])
                OD.dma(dbg["d_eq"], EQW[:, :], waits=[])
                OD.dma(dbg["d_cs"], CSW[:, :], waits=[])
                VEC.wait(sem_out, OD.cnt[0])
            VEC.wait(sem_a, ta_gw)
            tv_g = None
            for tc in range(8):
                t_ = VEC.mark(lambda e, tc=tc: e.tensor_scalar_mul(
                    CSW[:, tc * 8:(tc + 1) * 8], FF1[:, tc * 8:(tc + 1) * 8],
                    IV8[:, tc:tc + 1]))
            VEC.wait(sem_v, t_)
            for tc in range(8):
                tv_g = VEC.mark(lambda e, tc=tc: e.scalar_tensor_tensor(
                    GAT[:, tc * 8:(tc + 1) * 8], EQW[:, tc * 8:(tc + 1) * 8],
                    SM8[:, tc:tc + 1], CSW[:, tc * 8:(tc + 1) * 8],
                    ALU.mult, ALU.add))
            if debug and c == 0:
                GD.dma(dbg["d_gates"], GAT[:, :], waits=[(sem_v, tv_g)])
            gt_state = {"tok": None}
            rt_last = [None]
            rt_toks = [[None] * NDC, [None] * NDC]
            pre_ffn = {}

            def ffn_w(idx):
                if idx < 0:
                    return (dr["L0_ffn_sw1"], dr["L0_ffn_sw2"],
                            dr["L0_ffn_sw3"])
                return (dr["L0_ffn_ew1"][idx], dr["L0_ffn_ew2"][idx],
                        dr["L0_ffn_ew3"][idx])

            def issue_col(nm, wap, buf, hc):
                src = wap[:, hc * 128:(hc + 1) * 128].rearrange(
                    "(dc p) cc -> p dc cc", p=128)
                dst = buf[hc % 3][:, :].rearrange(
                    "p (dc cc) -> p dc cc", cc=128)
                w = []
                if rel.get((nm, hc % 3)) is not None:
                    w.append(rel[(nm, hc % 3)])
                rel[("dma", nm, hc % 3)] = GD.dma(dst, src, waits=w)

            def issue_e3(wap, dc):
                src = wap[:, dc * 128:(dc + 1) * 128].rearrange(
                    "(hh p) cc -> p hh cc", p=128)
                dst = WSA[dc % 2][:, 0:2048].rearrange(
                    "p (hh cc) -> p hh cc", cc=128)
                w = []
                if rel.get(("wsa", dc % 2)) is not None:
                    w.append(rel[("wsa", dc % 2)])
                rel[("dma", "e3", dc % 2)] = GD.dma(
                    dst[:, 0:8, :], src[:, 0:8, :], waits=w)
                rel[("dma", "e3b", dc % 2)] = GD.dma(
                    dst[:, 8:16, :], src[:, 8:16, :], waits=w)

            def preissue_unit(idx):
                w1ap, w2ap, _ = ffn_w(idx)
                for hh in (0, 1, 2):
                    issue_col("e1", w1ap, WSB, hh)
                    issue_col("e2", w2ap, WSC, hh)
                pre_ffn[idx] = True

            def emit_gates_transpose():
                kslot_t, trmk_g, trrel = trP.next()
                TEN.wait_tok(trrel)
                TEN.wait(sem_v, tv_g)
                tt_gt = None
                for tc in range(8):
                    fn = lambda e, tc=tc, trmk=trmk_g: e.transpose(
                        trmk()[0:8, tc * 128:(tc + 1) * 128],
                        GAT[:, tc * 8:(tc + 1) * 8], IDB[:, :])
                    if tc == 7:
                        tt_gt = TEN.mark(fn)
                    else:
                        TEN.do(fn)
                VEC.wait(sem_t, tt_gt)
                tv_gt = VEC.mark(lambda e, trmk=trmk_g: e.tensor_copy(
                    GT[:, :], trmk()[0:8, :]))
                trP.release(kslot_t, (sem_v, tv_gt))
                gt_state["tok"] = tv_gt

            preps = {}

            def prep_gates(idx):
                tvzg = [None, None]
                for j in (0, 1):
                    ks, gmk, grel = accP.next()
                    TEN.wait_tok(grel)
                    if j == 0:
                        TEN.wait(sem_v, gt_state["tok"])
                    tt_g = TEN.mark(lambda e, j=j, gmk=gmk, idx=idx: e.matmul(
                        gmk(), OHB[:, idx * 128:(idx + 1) * 128],
                        GT[0:8, JT[j]], start=True, stop=True))
                    VEC.wait(sem_t, tt_g)
                    tvg2 = VEC.mark(lambda e, j=j, gmk=gmk: e.tensor_copy(
                        GSB[:, JT[j]], gmk()))
                    accP.release(ks, (sem_v, tvg2))
                    for dc in range(NDC):
                        tvzg[j] = VEC.mark(lambda e, dc=dc, j=j: e.tensor_tensor(
                            B1[dc][:, JT[j]], B2[dc][:, JT[j]],
                            GSB[:, JT[j]], ALU.mult))
                preps[idx] = ((sem_v, tvzg[0]), (sem_v, tvzg[1]))
                rel[("gsbr",)] = (sem_v, tvzg[1])

            def ffn_unit(idx):
                shared = idx < 0
                w1ap, w2ap, w3ap = ffn_w(idx)
                if not shared:
                    if idx not in preps:
                        prep_gates(idx)
                    zg_tok = preps.pop(idx)
                    rhs2 = B1
                else:
                    rhs2 = B2
                    zg_tok = None
                he_last = None
                he_toks = [None] * NHC
                sil_i = [0]
                if not pre_ffn.pop(idx, False):
                    for hh in (0, 1, 2):
                        issue_col("e1", w1ap, WSB, hh)
                        issue_col("e2", w2ap, WSC, hh)
                for hc in range(NHC):
                    if hc == 10:
                        issue_e3(w3ap, 0)
                        issue_e3(w3ap, 1)
                    for th in (0, 1):
                        b1s, b1mk, b1rel = accP.next()
                        TEN.wait_tok(b1rel)
                        TEN.wait_tok(rel[("dma", "e1", hc % 3)])
                        if hc == 0:
                            TEN.wait_tok(az[th][NDC - 1])
                        t_p1 = None
                        for dc in range(NDC):
                            fn = lambda e, dc=dc, hc=hc, th=th, b1mk=b1mk: e.matmul(
                                b1mk(), WSB[hc % 3][:, dc * 128:(dc + 1) * 128],
                                B2[dc][:, JT[th]],
                                start=(dc == 0), stop=(dc == NDC - 1))
                            if dc == NDC - 1:
                                t_p1 = TEN.mark(fn)
                            else:
                                TEN.do(fn)
                        b2s, b2mk, b2rel = accP.next()
                        TEN.wait_tok(b2rel)
                        TEN.wait_tok(rel[("dma", "e2", hc % 3)])
                        if zg_tok is not None and hc == 0:
                            TEN.wait_tok(zg_tok[th])
                        t_p2 = None
                        for dc in range(NDC):
                            fn = lambda e, dc=dc, hc=hc, th=th, b2mk=b2mk, \
                                 rhs2=rhs2: e.matmul(
                                b2mk(), WSC[hc % 3][:, dc * 128:(dc + 1) * 128],
                                rhs2[dc][:, JT[th]],
                                start=(dc == 0), stop=(dc == NDC - 1))
                            if dc == NDC - 1:
                                t_p2 = TEN.mark(fn)
                            else:
                                TEN.do(fn)
                        if th == 1:
                            rel[("e1", hc % 3)] = (sem_t, t_p1)
                            rel[("e2", hc % 3)] = (sem_t, t_p2)
                            if hc + 3 < NHC:
                                issue_col("e1", w1ap, WSB, hc + 3)
                                issue_col("e2", w2ap, WSC, hc + 3)
                        ss = sil_i[0] % 2
                        sil_i[0] += 1
                        ACT.wait(sem_t, t_p1)
                        if rel.get(("sil", ss)) is not None:
                            ACT.wait_tok(rel[("sil", ss)])
                        if shared and hc == 0 and th == 0 and \
                           prev["chunk_pe"] is not None:
                            ACT.wait_tok(prev["chunk_pe"])
                        ta_s = ACT.mark(lambda e, ss=ss, b1mk=b1mk: e.activation(
                            SIL[ss][:, :], b1mk(), AF.Silu))
                        VEC.wait(sem_a, ta_s)
                        VEC.wait(sem_t, t_p2)
                        tv_he = VEC.mark(lambda e, hc=hc, th=th, ss=ss, b2mk=b2mk:
                                         e.tensor_tensor(B3[hc][:, JT[th]],
                                                         SIL[ss][:, :], b2mk(),
                                                         ALU.mult))
                        rel[("sil", ss)] = (sem_v, tv_he)
                        accP.release(b1s, (sem_v, tv_he))
                        accP.release(b2s, (sem_v, tv_he))
                        he_last = tv_he
                        he_toks[hc] = tv_he

                if shared:
                    emit_gates_transpose()
                for dc in range(NDC):
                    if dc == 2 and idx + 1 < NE:
                        prep_gates(idx + 1)
                    if dc == 4 and idx < NE - 1:
                        preissue_unit(idx + 1)
                    for th in (0, 1):
                        bs, bmk2, brel2 = accQ.next()
                        TEN.wait_tok(brel2)
                        TEN.wait_tok(rel[("dma", "e3", dc % 2)])
                        t_eo = None
                        for hc in range(NHC):
                            if hc == 8:
                                TEN.wait_tok(rel[("dma", "e3b", dc % 2)])
                            if dc == 0 and th == 0:
                                TEN.wait(sem_v, he_toks[hc])
                            fn = lambda e, hc=hc, dc=dc, th=th, bmk2=bmk2: \
                                e.matmul(bmk2(),
                                         WSA[dc % 2][:, hc * 128:(hc + 1) * 128],
                                         B3[hc][:, JT[th]],
                                         start=(hc == 0), stop=(hc == NHC - 1))
                            if hc == NHC - 1:
                                t_eo = TEN.mark(fn)
                            else:
                                TEN.do(fn)
                        if th == 1:
                            rel[("wsa", dc % 2)] = (sem_t, t_eo)
                            if dc + 2 < NDC:
                                issue_e3(w3ap, dc + 2)
                        VEC.wait(sem_t, t_eo)
                        if shared:
                            tv_rt = VEC.mark(lambda e, dc=dc, th=th, bmk2=bmk2:
                                             e.tensor_copy(FX[dc][:, JT[th]],
                                                           bmk2()))
                        else:
                            tv_rt = VEC.mark(lambda e, dc=dc, th=th, bmk2=bmk2:
                                             e.tensor_tensor(FX[dc][:, JT[th]],
                                                             FX[dc][:, JT[th]],
                                                             bmk2(), ALU.add))
                        accQ.release(bs, (sem_v, tv_rt))
                        rt_last[0] = tv_rt
                        if idx == NE - 1:
                            rt_toks[th][dc] = (sem_v, tv_rt)

            for idx in range(-1, NE):
                ffn_unit(idx)
                if debug and c == 0 and idx == -1:
                    for dc in range(NDC):
                        OD.dma(dbg["d_rts"][dc * 128:(dc + 1) * 128, :],
                               FX[dc][:, :], waits=[(sem_v, rt_last[0])])

            if debug and c == 0:
                for dc in range(NDC):
                    OD.dma(dbg["d_rt"][dc * 128:(dc + 1) * 128, :],
                           FX[dc][:, :], waits=[(sem_v, rt_last[0])])

            prev["chunk_pe"] = TEN.tok()

            aout, tout, out_lasts = emit_ln(
                lambda dc: FX[dc][:, :],
                mk_apply(FX, FA, "res", res_buf=FB, dst_waits=rt_toks,
                         order="dc"),
                src_toks=rt_toks)
            prev["lnout_twins"] = tout
            prev["lnout_apply"] = aout

            out_toks = []
            for dc in range(NDC):
                ot = OD.dma(outT[dc * 128:(dc + 1) * 128, csl], FA[dc][:, :],
                            waits=[aout[1][dc]])
                out_toks.append(ot)
            prev["out_toks"] = out_toks

        OD.wait_all(SYNC)
        GD.wait_all(SYNC)
        SD.wait_all(SYNC)

        with nc.Block() as block:
            @block.tensor
            def _(t):
                TEN.play(t)

            @block.vector
            def _(v):
                VEC.play(v)

            @block.scalar
            def _(s):
                ACT.play(s)

            @block.gpsimd
            def _(g):
                POOL.play(g)

            @block.sync
            def _(sy):
                SYNC.play(sy)

    return nc


def _consts():
    c = {}
    c["ident_b"] = np.eye(128, dtype=ml_dtypes.bfloat16)
    c["onec_b"] = np.ones((128, 1), dtype=ml_dtypes.bfloat16)
    c["oner_b"] = np.ones((1, 128), dtype=ml_dtypes.bfloat16)
    c["moner_b"] = -np.ones((1, 128), dtype=ml_dtypes.bfloat16)
    c["oner_f"] = np.ones((1, 128), dtype=np.float32)
    oh = np.zeros((8, 8 * 128), dtype=ml_dtypes.bfloat16)
    for e in range(8):
        oh[e, e * 128:(e + 1) * 128] = 1
    c["oh_b"] = oh
    c["ones8_f"] = np.ones((128, 8), dtype=np.float32)
    return c


def _col128(v):
    return np.ascontiguousarray(np.asarray(v).reshape(8, 128).T)


def make_inputs(inp):
    vec_pack = np.zeros((128, 64), dtype=np.float32)
    vec_pack[:, 0:8] = _col128(inp["L0_attn_log_lambda"])
    vec_pack[:, 8:16] = _col128(inp["L0_attn_D_skip"])
    vec_pack[:, 16:24] = _col128(inp["L0_attn_ln1_g"])
    vec_pack[:, 24:32] = _col128(inp["L0_attn_ln1_b"])
    vec_pack[:, 32:40] = _col128(inp["L0_attn_ln2_g"])
    vec_pack[:, 40:48] = _col128(inp["L0_attn_ln2_b"])
    vec_pack[:, 48:56] = _col128(inp["L0_attn_b2"])
    b1_pack = np.ascontiguousarray(np.asarray(inp["L0_attn_b1"]).reshape(16, 128).T)
    rb_row = np.ascontiguousarray(np.asarray(inp["L0_ffn_rb"]).reshape(1, 8))
    shared = {k: np.asarray(inp[k]) for k in (
        "L0_attn_w_in", "L0_attn_w_out", "L0_attn_w1", "L0_attn_w2",
        "L0_ffn_sw1", "L0_ffn_sw2", "L0_ffn_sw3",
        "L0_ffn_ew1", "L0_ffn_ew2", "L0_ffn_ew3", "L0_ffn_rw")}
    shared["vec_pack"] = vec_pack
    shared["b1_pack"] = b1_pack
    shared["rb_row"] = rb_row
    shared.update(_consts())
    x = np.asarray(inp["L0_attn_x"])
    in_maps = []
    for b in range(x.shape[0]):
        m = dict(shared)
        m["xT"] = np.ascontiguousarray(x[b].T)
        in_maps.append(m)
    return in_maps


def kernel(**inputs):
    in_maps = make_inputs(inputs)
    nc = build_program(debug=False)
    res = run_bass_kernel_spmd(nc, in_maps, core_ids=list(range(8)))
    out = np.stack([np.ascontiguousarray(np.asarray(r["outT"]).T)
                    for r in res.results], axis=0)
    return out.astype(np.float32)
